# version 01
# baseline (speedup 1.0000x reference)
import numpy as np
from contextlib import ExitStack

import concourse.bass as bass
import concourse.mybir as mybir
from concourse.bass_utils import run_bass_kernel_spmd

F32 = mybir.dt.float32
BF16 = mybir.dt.bfloat16
AF = mybir.ActivationFunctionType
ALU = mybir.AluOpType
AXX = mybir.AxisListType.X

S = 2048
D = 1024
DIN = 2048
DPROJ = 4384
CH = 256
NCH = S // CH
FD = 4096
EPS = 1e-5
RSQ32 = float(1.0 / np.sqrt(np.float32(32.0)))
XBCW = 512 + 3

_DEBUG = 0
_UPTO = "all"


class Sem:
    def __init__(self, nc, ctx, name):
        self.h = ctx.enter_context(nc.semaphore(name))
        self.v = 0

    def inc(self, inst, k=1):
        self.v += k
        inst.then_inc(self.h, k)
        return self.v

    def dma(self, inst):
        return self.inc(inst, 16)

    def wait(self, eng, v):
        if v > 0:
            eng.wait_ge(self.h, v)



class EP:

    def __init__(self, eng):
        object.__setattr__(self, "_e", eng)
        object.__setattr__(self, "last", None)

    def __getattr__(self, n):
        f = getattr(self._e, n)
        if not callable(f):
            return f

        def w(*a, **k):
            r = f(*a, **k)
            if n != "wait_ge" and r is not None:
                object.__setattr__(self, "last", r)
            return r
        return w


class Builder:
    def __init__(self, debug=0, upto="all"):
        self.debug = debug
        self.upto = upto
        self.C = {}

    def t(self, fn): self.blk.tensor(lambda e: fn(EP(e)))
    def v(self, fn): self.blk.vector(lambda e: fn(EP(e)))
    def s(self, fn): self.blk.scalar(lambda e: fn(EP(e)))
    def g(self, fn): self.blk.gpsimd(lambda e: fn(EP(e)))
    def sp(self, fn): self.blk.sync(lambda e: fn(EP(e)))

    def build(self):
        nc = bass.Bass("TRN2", target_bir_lowering=False, detect_race_conditions=False)
        self.nc = nc
        C = self.C

        dram = lambda n, sh, dd=F32, kind="ExternalInput": nc.dram_tensor(n, sh, dd, kind=kind).ap()
        self.xT = dram("xT", [D, S])
        self.w_inT = dram("w_inT", [D, DPROJ])
        self.convw = dram("convw", [128, 72])
        self.convb = dram("convb", [128, 18])
        self.dtbias = dram("dtbias", [32, 1])
        self.alog = dram("alog", [32, 1])
        self.drepd = dram("drepd", [128, 16])
        self.normwd = dram("normwd", [128, 16])
        self.w_outT = dram("w_outT", [DIN, D])
        self.lngd = dram("lngd", [128, 8])
        self.lnbd = dram("lnbd", [128, 8])
        self.wq = dram("wq", [D, D]); self.wk = dram("wk", [D, D])
        self.wv = dram("wv", [D, D]); self.wo = dram("wo", [D, D])
        self.w1 = dram("w1", [2, D, FD]); self.w2 = dram("w2", [2, FD, D])
        self.b1t = dram("b1t", [2, 128, 32]); self.b2t = dram("b2t", [2, 128, 8])
        self.cid = dram("cid", [128, 128])
        self.ctri = dram("ctri", [128, 128])
        self.outT = dram("outT", [D, S], F32, "ExternalOutput")
        self.s_z = dram("s_z", [16, 128, S], BF16, "Internal")
        self.s_t = dram("s_t", [16, 128, S], BF16, "Internal")
        self.s_m = dram("s_m", [8, 128, S], BF16, "Internal")
        self.s_o = dram("s_o", [8, 128, S], BF16, "Internal")
        self.s_rows = dram("s_rows", [8, S], F32, "Internal")
        self.s_cs = dram("s_cs", [32, S], F32, "Internal")
        self.s_sums = dram("s_sums", [32, S], F32, "Internal")

        self.dbg = {}
        if self.debug >= 1:
            def dbgt(name, sh):
                self.dbg[name] = dram("dbg_" + name, sh, F32, "ExternalOutput")
            dbgt("xn", [8, 128, S])
            dbgt("dtp", [32, S])
            dbgt("cs", [32, S])
            dbgt("dd", [32, S])
            dbgt("xbc", [18, 128, 256])
            dbgt("t", [16, 128, S])
            dbgt("m", [8, 128, S])
            dbgt("rstd", [5, S])
            dbgt("xn2", [8, 128, S])
            dbgt("ao", [8, 128, S])
            dbgt("o", [8, 128, S])
            dbgt("mt", [2, 128, 384])
            dbgt("mv", [2, 128, 256])
            dbgt("lsb", [2, 128, 384])
            dbgt("csb", [2, 128, 256])
            dbgt("xtok", [2, 128, 2048])
            dbgt("gm", [128, 384])
            dbgt("bmt", [128, 256])
            dbgt("colt", [128, 16 * 96])
            dbgt("ecs", [2, 128, 256])
            dbgt("ytmp", [128, 256])
            dbgt("uall", [128, 4096])
            dbgt("xpost", [128, 4096])

        with ExitStack() as top:
            TE = top.enter_context
            self.colT = TE(nc.sbuf_tensor("colT", [128, 16 * 96], F32))
            self.ones_c = TE(nc.sbuf_tensor("ones_c", [128, 1], F32))
            self.ones_cb = TE(nc.sbuf_tensor("ones_cb", [128, 1], BF16))
            self.epsc = TE(nc.sbuf_tensor("epsc", [128, 1], F32))
            self.cb4T = TE(nc.sbuf_tensor("cb4T", [128, 4], F32))
            self.idb = TE(nc.sbuf_tensor("idb", [128, 128], BF16))
            self.idf = TE(nc.sbuf_tensor("idf", [128, 128], F32))
            self.trif = TE(nc.sbuf_tensor("trif", [128, 128], F32))
            self.vecs = TE(nc.sbuf_tensor("vecs", [128, 64], F32))
            self.convb_sb = TE(nc.sbuf_tensor("convb_sb", [128, 18], F32))
            self.convbh = TE(nc.sbuf_tensor("convbh", [128, 18], F32))
            self.convw_sb = TE(nc.sbuf_tensor("convw_sb", [128, 72], F32))
            self.smv = TE(nc.sbuf_tensor("smv", [32, 8], F32))
            self.b1_sb = TE(nc.sbuf_tensor("b1_sb", [128, 64], F32))
            self.b2_sb = TE(nc.sbuf_tensor("b2_sb", [128, 16], F32))

            self.sW = Sem(nc, top, "sW")
            self.sIn = Sem(nc, top, "sIn")
            self.sSc = Sem(nc, top, "sSc")
            self.sPE = Sem(nc, top, "sPE")
            self.sDV = Sem(nc, top, "sDV")
            self.sAC = Sem(nc, top, "sAC")
            self.sOut = Sem(nc, top, "sOut")

            self.blk = TE(nc.Block())
            self.xn = TE(nc.sbuf_tensor("xn", [128, 8 * S], BF16))

            with ExitStack() as scA:
                self.phase_p0(scA.enter_context)
                self.phase_p1(scA.enter_context)
                if self.upto == "p1":
                    self.finish_dummy()
                    return nc, self.dbg
                self.phase_m1_m2(scA.enter_context)
            with ExitStack() as scB:
                self.phase_m3(scB.enter_context)
            if self.upto == "m":
                self.finish_dummy()
                return nc, self.dbg
            with ExitStack() as scC:
                self.phase_attn(scC.enter_context)
            if self.upto == "f":
                self.finish_dummy()
                return nc, self.dbg
            with ExitStack() as scD:
                self.phase_moe_p5(scD.enter_context)

            def fin(sp):
                self.sOut.wait(sp, self.sOut.v)
            self.sp(fin)
        return nc, self.dbg

    def finish_dummy(self):
        nc = self.nc
        with ExitStack() as sc:
            z = sc.enter_context(nc.sbuf_tensor("zfin", [128, 512], F32))

            def v_fin(v):
                self.sDV.wait(v, self.sDV.v)
                self.sAC.wait(v, self.sAC.v)
                self.sPE.wait(v, self.sPE.v)
                self.sIn.wait(v, self.sIn.v)
                self.sSc.wait(v, self.sSc.v)
                self.sW.wait(v, self.sW.v)
                v.memset(z[:, :], 0.0)
                self.C["fin_z"] = self.sDV.inc(v.last)
            self.v(v_fin)

            def sp_fin(sp):
                self.sDV.wait(sp, self.C["fin_z"])
                for k in range(8):
                    for tc in range(4):
                        self.sOut.dma(sp.dma_start(
                            out=self.outT[k * 128:(k + 1) * 128, tc * 512:(tc + 1) * 512], in_=z[:, :]))
                self.sOut.wait(sp, self.sOut.v)
            self.sp(sp_fin)

    def phase_p0(self, AE):
        nc, C = self.nc, self.C
        sW, sIn, sSc, sPE, sDV, sAC = self.sW, self.sIn, self.sSc, self.sPE, self.sDV, self.sAC
        self.w_in_sb = AE(nc.sbuf_tensor("w_in_sb", [128, 8 * DPROJ], BF16))
        xn = self.xn
        self.p0_stack = ExitStack()
        P0 = self.p0_stack.enter_context
        xbuf = P0(nc.sbuf_tensor("xbuf", [128, 2 * S], F32))
        sqb = P0(nc.sbuf_tensor("sqb", [128, 2 * S], F32))
        mu_r = P0(nc.sbuf_tensor("mu_r", [1, S], F32))
        var_r = P0(nc.sbuf_tensor("var_r", [1, S], F32))
        tmp_r = P0(nc.sbuf_tensor("tmp_r", [1, S], F32))
        bcb = P0(nc.sbuf_tensor("bcb", [128, 2 * S], F32))
        ps_sum = P0(nc.psum_tensor("ps_sum", [1, S], F32))
        ps_sq = P0(nc.psum_tensor("ps_sq", [1, S], F32))

        def g0(g):
            for dst, src in [(self.idb, self.cid), (self.idf, self.cid), (self.trif, self.ctri),
                             (self.convw_sb, self.convw), (self.convb_sb, self.convb)]:
                sW.dma(g.dma_start(out=dst[:, :], in_=src[:, :]))
            sW.dma(g.dma_start(out=self.vecs[:, 0:16], in_=self.drepd[:, :]))
            sW.dma(g.dma_start(out=self.vecs[:, 16:32], in_=self.normwd[:, :]))
            sW.dma(g.dma_start(out=self.vecs[:, 32:40], in_=self.lngd[:, :]))
            sW.dma(g.dma_start(out=self.vecs[:, 40:48], in_=self.lnbd[:, :]))
            sW.dma(g.dma_start(out=self.smv[:, 0:1], in_=self.dtbias[:, :]))
            sW.dma(g.dma_start(out=self.smv[:, 2:3], in_=self.alog[:, :]))
            sW.dma(g.dma_start(out=self.b1_sb[:, 0:32], in_=self.b1t[0, :, :]))
            sW.dma(g.dma_start(out=self.b1_sb[:, 32:64], in_=self.b1t[1, :, :]))
            sW.dma(g.dma_start(out=self.b2_sb[:, 0:8], in_=self.b2t[0, :, :]))
            sW.dma(g.dma_start(out=self.b2_sb[:, 8:16], in_=self.b2t[1, :, :]))
            C["consts_w"] = sW.v
            for k in range(8):
                sW.dma(g.dma_start(out=self.w_in_sb[:, k * DPROJ:(k + 1) * DPROJ],
                                   in_=self.w_inT[k * 128:(k + 1) * 128, :]))
            C["w_in_w"] = sW.v
        self.g(g0)

        def v00(v):
            v.memset(self.ones_c[:, :], 1.0)
            v.memset(self.ones_cb[:, :], 1.0)
            v.memset(self.epsc[:, :], EPS)
            v.memset(self.cb4T[:, :], 0.0)
            for j in range(4):
                v.memset(self.cb4T[32 * j:32 * (j + 1), j:j + 1], 1.0)
            sW.wait(v, C["consts_w"])
            v.tensor_scalar_mul(self.convbh[:, :], self.convb_sb[:, :], 0.5)
            C["consts_v"] = sDV.inc(v.last)
        self.v(v00)

        for k in range(8):
            def s_x1(sp, k=k):
                if k >= 2:
                    sPE.wait(sp, C["p0_mm_%d" % (k - 2)])
                C["p0_x1_%d" % k] = sIn.dma(
                    sp.dma_start(out=xbuf[:, (k % 2) * S:(k % 2) * S + S],
                                 in_=self.xT[k * 128:(k + 1) * 128, :]))
            self.sp(s_x1)

            def a_sq(s, k=k):
                sIn.wait(s, C["p0_x1_%d" % k])
                if k >= 2:
                    sPE.wait(s, C["p0_mm_%d" % (k - 2)])
                s.activation(sqb[:, (k % 2) * S:(k % 2) * S + S], xbuf[:, (k % 2) * S:(k % 2) * S + S],
                             AF.Square)
                C["p0_sq_%d" % k] = sAC.inc(s.last)
            self.s(a_sq)

            def t_mm(t, k=k):
                if k == 0:
                    sDV.wait(t, C["consts_v"])
                sIn.wait(t, C["p0_x1_%d" % k])
                sAC.wait(t, C["p0_sq_%d" % k])
                for tc in range(4):
                    t.matmul(ps_sum[0:1, tc * 512:(tc + 1) * 512], self.ones_c[:, :],
                             xbuf[:, (k % 2) * S + tc * 512:(k % 2) * S + tc * 512 + 512],
                             start=(k == 0), stop=(k == 7))
                    t.matmul(ps_sq[0:1, tc * 512:(tc + 1) * 512], self.ones_c[:, :],
                             sqb[:, (k % 2) * S + tc * 512:(k % 2) * S + tc * 512 + 512],
                             start=(k == 0), stop=(k == 7))
                C["p0_mm_%d" % k] = sPE.inc(t.last)
            self.t(t_mm)

        def v_ev(v):
            sPE.wait(v, C["p0_mm_7"])
            v.tensor_scalar_mul(mu_r[0:1, :], ps_sum[0:1, :], 1.0 / D)
            v.tensor_scalar_mul(var_r[0:1, :], ps_sq[0:1, :], 1.0 / D)
            v.tensor_mul(tmp_r[0:1, :], mu_r[0:1, :], mu_r[0:1, :])
            v.tensor_sub(var_r[0:1, :], var_r[0:1, :], tmp_r[0:1, :])
            C["p0_ev"] = sDV.inc(v.last)
        self.v(v_ev)

        def a_rs(s):
            sDV.wait(s, C["p0_ev"])
            s.activation(var_r[0:1, :], var_r[0:1, :], AF.Ln, bias=self.epsc[0:1, 0:1])
            s.activation(var_r[0:1, :], var_r[0:1, :], AF.Exp, scale=-0.5)
            C["p0_rs"] = sAC.inc(s.last)
        self.s(a_rs)

        def s_rw(sp):
            sAC.wait(sp, C["p0_rs"])
            sIn.dma(sp.dma_start(out=self.s_rows[0:1, :], in_=mu_r[0:1, :]))
            C["p0_rw"] = sIn.dma(sp.dma_start(out=self.s_rows[1:2, :], in_=var_r[0:1, :]))
        self.sp(s_rw)

        for k in range(8):
            def s_bc(sp, k=k):
                if k == 0:
                    sIn.wait(sp, C["p0_rw"])
                    sIn.dma(sp.dma_start(out=bcb[:, 0:S],
                                         in_=self.s_rows[0:1, :].broadcast_to([128, S])))
                    C["p0_bc"] = sIn.dma(
                        sp.dma_start(out=bcb[:, S:2 * S],
                                     in_=self.s_rows[1:2, :].broadcast_to([128, S])))
                if k < 2:
                    sPE.wait(sp, C["p0_mm_7"])
                    sAC.wait(sp, C["p0_sq_7"])
                else:
                    sDV.wait(sp, C["p0_xn_%d" % (k - 2)])
                C["p0_x2_%d" % k] = sIn.dma(
                    sp.dma_start(out=xbuf[:, (k % 2) * S:(k % 2) * S + S],
                                 in_=self.xT[k * 128:(k + 1) * 128, :]))
            self.sp(s_bc)

            def v_xn(v, k=k):
                sIn.wait(v, C["p0_x2_%d" % k])
                sIn.wait(v, C["p0_bc"])
                if k >= 1:
                    sAC.wait(v, C["p0_sq2_%d" % (k - 1)])
                sl = slice(k * S, (k + 1) * S)
                xb = xbuf[:, (k % 2) * S:(k % 2) * S + S]
                v.tensor_sub(sqb[:, 0:S], xb, bcb[:, 0:S])
                v.tensor_mul(xn[:, sl], sqb[:, 0:S], bcb[:, S:2 * S])
                C["p0_xn_%d" % k] = sDV.inc(v.last)
            self.v(v_xn)

            def a_sq2(s, k=k):
                sDV.wait(s, C["p0_xn_%d" % k])
                if k >= 1:
                    sPE.wait(s, C["p0_mm2_%d" % (k - 1)])
                s.activation(sqb[:, S:2 * S], xn[:, k * S:(k + 1) * S], AF.Square)
                C["p0_sq2_%d" % k] = sAC.inc(s.last)
            self.s(a_sq2)

            def t_mm2(t, k=k):
                sAC.wait(t, C["p0_sq2_%d" % k])
                for tc in range(4):
                    t.matmul(ps_sum[0:1, tc * 512:(tc + 1) * 512], self.ones_c[:, :],
                             sqb[:, S + tc * 512: S + tc * 512 + 512],
                             start=(k == 0), stop=(k == 7))
                C["p0_mm2_%d" % k] = sPE.inc(t.last)
            self.t(t_mm2)

        def v_ev2(v):
            sPE.wait(v, C["p0_mm2_7"])
            v.tensor_scalar_mul(tmp_r[0:1, :], ps_sum[0:1, :], 1.0 / D)
            C["p0_ev2"] = sDV.inc(v.last)
        self.v(v_ev2)

        def a_rs2(s):
            sDV.wait(s, C["p0_ev2"])
            s.activation(tmp_r[0:1, :], tmp_r[0:1, :], AF.Ln, bias=self.epsc[0:1, 0:1])
            s.activation(tmp_r[0:1, :], tmp_r[0:1, :], AF.Exp, scale=-0.5)
            C["p0_rs2"] = sAC.inc(s.last)
        self.s(a_rs2)

        def s_rw2(sp):
            sAC.wait(sp, C["p0_rs2"])
            C["p0_rw2"] = sIn.dma(sp.dma_start(out=self.s_rows[2:3, :], in_=tmp_r[0:1, :]))
        self.sp(s_rw2)

        C["p0_dv"] = sDV.v
        C["p0_ac"] = sAC.v
        C["p0_pe"] = sPE.v

        if self.debug >= 1:
            def gdbg(g):
                sDV.wait(g, C["p0_dv"])
                sAC.wait(g, C["p0_rs2"])
                for k in range(8):
                    self.sW.dma(g.dma_start(out=self.dbg["xn"][k, :, :], in_=xn[:, k * S:(k + 1) * S]))
                self.sW.dma(g.dma_start(out=self.dbg["rstd"][0:1, :], in_=var_r[0:1, :]))
                self.sW.dma(g.dma_start(out=self.dbg["rstd"][1:2, :], in_=tmp_r[0:1, :]))
            self.g(gdbg)

    def phase_p1(self, AE):
        nc, C = self.nc, self.C
        sW, sIn, sSc, sPE, sDV, sAC = self.sW, self.sIn, self.sSc, self.sPE, self.sDV, self.sAC
        xn = self.xn
        self.p0_stack.close()
        with ExitStack() as p1s:
            PP = p1s.enter_context
            csR = PP(nc.sbuf_tensor("csR", [32, S], F32))
            ddR = PP(nc.sbuf_tensor("ddR", [32, S], F32))
            dtpR = PP(nc.sbuf_tensor("dtpR", [32, S], F32))
            esb = PP(nc.sbuf_tensor("esb", [32, S], F32))
            ps_dt = PP(nc.psum_tensor("ps_dt", [32, S], F32))
            ps_t1 = PP(nc.psum_tensor("ps_t1", [128, 512], F32))
            ps_t2 = PP(nc.psum_tensor("ps_t2", [128, 512], F32))

            def t_p1a(t):
                sW.wait(t, C["w_in_w"])
                sDV.wait(t, C["p0_dv"])
                for kt in range(8):
                    for tc in range(4):
                        t.matmul(ps_dt[:, tc * 512:(tc + 1) * 512],
                                 self.w_in_sb[:, kt * DPROJ + 4352: kt * DPROJ + 4384],
                                 xn[:, kt * S + tc * 512: kt * S + tc * 512 + 512],
                                 start=(kt == 0), stop=(kt == 7))
                C["p1_dt"] = sPE.inc(t.last)
            self.t(t_p1a)

            def a_p1(s):
                sPE.wait(s, C["p1_dt"])
                s.activation(esb[:, :], ps_dt[:, :], AF.Exp, bias=self.smv[:, 0:1])
                s.activation(dtpR[:, :], esb[:, :], AF.Ln, bias=self.ones_c[0:32, 0:1])
                s.activation(self.smv[:, 1:2], self.smv[:, 2:3], AF.Exp)
                C["p1_dtp"] = sAC.inc(s.last)
            self.s(a_p1)

            def v_p1(v):
                sAC.wait(v, C["p1_dtp"])
                v.tensor_scalar(esb[:, :], dtpR[:, :], self.smv[:, 1:2], -1.0, ALU.mult, ALU.mult)
                for c in range(NCH):
                    v.tensor_tensor_scan(csR[:, c * CH:(c + 1) * CH],
                                         esb[:, c * CH:(c + 1) * CH], esb[:, c * CH:(c + 1) * CH],
                                         0.0, ALU.add, ALU.bypass)
                for c in range(NCH):
                    v.tensor_scalar(ddR[:, c * CH:(c + 1) * CH],
                                    csR[:, c * CH:(c + 1) * CH],
                                    csR[:, c * CH + CH - 1: c * CH + CH], None, ALU.subtract)
                C["p1_cs"] = sDV.inc(v.last)
            self.v(v_p1)

            def a_p1b(s):
                sDV.wait(s, C["p1_cs"])
                s.activation(ddR[:, :], ddR[:, :], AF.Exp, scale=-1.0)
                C["p1_dec"] = sAC.inc(s.last)
            self.s(a_p1b)

            def v_p1b(v):
                sAC.wait(v, C["p1_dec"])
                v.tensor_mul(ddR[:, :], ddR[:, :], dtpR[:, :])
                C["p1_dd"] = sDV.inc(v.last)
            self.v(v_p1b)

            def s_cs(sp):
                sDV.wait(sp, C["p1_dd"])
                C["p1_csw"] = sIn.dma(sp.dma_start(out=self.s_cs[:, :], in_=csR[:, :]))
            self.sp(s_cs)

            srcRs = [csR, ddR, dtpR]
            for blki in range(16):
                def t_tr(t, blki=blki):
                    if blki == 0:
                        sDV.wait(t, C["p1_dd"])
                    c, st = blki // 2, blki % 2
                    ps = ps_t1 if blki % 2 == 0 else ps_t2
                    if blki >= 2:
                        sDV.wait(t, C["p1_ct_%d" % (blki - 2)])
                    for ri, rt in enumerate(srcRs):
                        t.transpose(ps[:, ri * 32:(ri + 1) * 32],
                                    rt[:, c * CH + st * 128: c * CH + st * 128 + 128],
                                    self.idf[0:32, 0:32])
                    C["p1_tr_%d" % blki] = sPE.inc(t.last)
                self.t(t_tr)

                def v_tr(v, blki=blki):
                    ps = ps_t1 if blki % 2 == 0 else ps_t2
                    sPE.wait(v, C["p1_tr_%d" % blki])
                    v.tensor_copy(self.colT[:, blki * 96:(blki + 1) * 96], ps[:, 0:96])
                    C["p1_ct_%d" % blki] = sDV.inc(v.last)
                self.v(v_tr)

            if self.debug >= 1:
                def gdbg1(g):
                    sDV.wait(g, C["p1_dd"])
                    self.sW.dma(g.dma_start(out=self.dbg["dtp"][:, :], in_=dtpR[:, :]))
                    self.sW.dma(g.dma_start(out=self.dbg["cs"][:, :], in_=csR[:, :]))
                    self.sW.dma(g.dma_start(out=self.dbg["dd"][:, :], in_=ddR[:, :]))
                    self.sDV.wait(g, C["p1_ct_15"])
                self.g(gdbg1)
        C["p1_dv"] = sDV.v
        C["p1_ac"] = sAC.v
        C["p1_pe"] = sPE.v

    def phase_m1_m2(self, AE):
        nc, C = self.nc, self.C
        sDV, sSc, sAC, sIn = self.sDV, self.sSc, self.sAC, self.sIn

        self.xbc = AE(nc.sbuf_tensor("xbc", [128, 18 * XBCW], BF16))
        self.zstg = AE(nc.sbuf_tensor("zstg", [128, 4 * 512], BF16))
        self.zc = AE(nc.sbuf_tensor("zc", [128, 16 * CH], BF16))
        self.thz = AE(nc.sbuf_tensor("thz", [128, 8 * CH], BF16))
        self.uall = AE(nc.sbuf_tensor("uall", [128, 16 * CH], BF16))
        self.xpost = AE(nc.sbuf_tensor("xpost", [128, 16 * CH], BF16))
        self.bct = AE(nc.sbuf_tensor("bct", [128, 2 * CH], BF16))
        self.x_tok = [AE(nc.sbuf_tensor("x_tok%d" % i, [128, DIN], BF16)) for i in range(2)]
        self.Bmt = AE(nc.sbuf_tensor("Bmt", [128, 2 * 128], BF16))
        self.carry = AE(nc.sbuf_tensor("carry", [128, DIN], F32))
        self.carry_bf = AE(nc.sbuf_tensor("carry_bf", [128, DIN], BF16))
        self.Gm = AE(nc.sbuf_tensor("Gm", [128, 384], BF16))
        self.pre_c = AE(nc.sbuf_tensor("pre_c", [128, 2 * CH], F32))
        self.thc = AE(nc.sbuf_tensor("thc", [128, 2 * CH], BF16))
        self.Lsb = AE(nc.sbuf_tensor("Lsb", [128, 2 * 384], BF16))
        self.Mt_sb = AE(nc.sbuf_tensor("Mt_sb", [128, 2 * 384], BF16))
        self.Mv_sb = AE(nc.sbuf_tensor("Mv_sb", [128, 2 * CH], BF16))
        self.Ecsb = AE(nc.sbuf_tensor("Ecsb", [128, 2 * CH], BF16))
        self.xddb = AE(nc.sbuf_tensor("xddb", [128, 2 * 128], BF16))
        self.tbc = AE(nc.sbuf_tensor("tbc", [128, 16 * CH], BF16))
        self.sqt = AE(nc.sbuf_tensor("sqt", [128, 2 * CH], BF16))
        self.ytmp = AE(nc.sbuf_tensor("ytmp", [128, CH], F32))
        self.csb = AE(nc.sbuf_tensor("csb", [128, 2 * CH], F32))
        self.cdb = AE(nc.sbuf_tensor("cdb", [128, 256], F32))
        self.msrow = AE(nc.sbuf_tensor("msrow", [1, CH], F32))

        self.psA = AE(nc.psum_tensor("psA", [128, 512], F32))
        self.psB = AE(nc.psum_tensor("psB", [128, 512], F32))
        self.psC = AE(nc.psum_tensor("psC", [128, 512], F32))
        self.psD = AE(nc.psum_tensor("psD", [128, 512], F32))
        self.psE = AE(nc.psum_tensor("psE", [128, 1024], BF16))
        self.psF = AE(nc.psum_tensor("psF", [128, 1024], BF16))

        def s_cdb(sp):
            sIn.wait(sp, C["p1_csw"])
            C["cdb_l"] = sSc.dma(sp.dma_start(
                out=self.cdb[:, :].rearrange("p (h c) -> p h c", h=32),
                in_=self.s_cs[:, CH - 1::CH].rearrange("h c -> () h c").broadcast_to([128, 32, 8]),
                allow_slow_non_contiguous=True))
        self.sp(s_cdb)

        def a_cdb(s):
            sSc.wait(s, C["cdb_l"])
            s.activation(self.cdb[:, :], self.cdb[:, :], AF.Exp)
            C["cdb"] = sAC.inc(s.last)
        self.s(a_cdb)

        def v_init(v):
            if self.debug:
                self.sW.wait(v, self.sW.v)
            v.memset(self.carry[:, :], 0.0)
            v.memset(self.carry_bf[:, :], 0.0)
            for t18 in range(18):
                v.memset(self.xbc[:, t18 * XBCW: t18 * XBCW + 3], 0.0)
            C["m2_init"] = sDV.inc(v.last)
        self.v(v_init)
        for h in range(32):
            C["carry_-1_%d" % h] = C["m2_init"]

        for tc in range(4):
            self._emit_m1_tc(tc)
            for c in (2 * tc, 2 * tc + 1):
                self._emit_m2_chunk(c)
            if tc < 3:
                def v_hist(v, tc=tc):
                    for t18 in range(18):
                        v.tensor_copy(self.xbc[:, t18 * XBCW: t18 * XBCW + 3],
                                      self.xbc[:, t18 * XBCW + 512: t18 * XBCW + 515])
                    C["m2_hist_%d" % tc] = sDV.inc(v.last)
                self.v(v_hist)

        C["m2_dv"] = sDV.v
        C["m2_pe"] = self.sPE.v
        C["m2_ac"] = self.sAC.v
        C["m2_td"] = C["m2_tdma_%d" % (NCH - 1)]
        C["m2_msw"] = C["msw_%d" % (NCH - 1)]

    def _emit_m1_tc(self, tc):
        nc, C = self.nc, self.C
        sW, sIn, sPE, sDV, sAC = self.sW, self.sIn, self.sPE, self.sDV, self.sAC
        xn, w_in_sb = self.xn, self.w_in_sb
        xbc, zstg = self.xbc, self.zstg
        ps_zx = [self.psA, self.psB]

        for mt in range(34):
            def t_m1(t, mt=mt, tc=tc):
                i = tc * 34 + mt
                if i == 0:
                    sAC.wait(t, C["p1_ac"])
                    sDV.wait(t, C["p1_dv"])
                if mt == 0 and tc > 0:
                    sDV.wait(t, C["carry_%d_%d" % (2 * tc - 1, 31)])
                    sDV.wait(t, C["tpair_ev_%d_%d" % (2 * tc - 1, 14)])
                    sDV.wait(t, C["tpair_ev_%d_%d" % (2 * tc - 1, 15)])
                if i >= 2:
                    sDV.wait(t, C["m1_ev_%d" % (i - 2)])
                ps = ps_zx[i % 2]
                for kt in range(8):
                    t.matmul(ps[:, :],
                             w_in_sb[:, kt * DPROJ + mt * 128: kt * DPROJ + (mt + 1) * 128],
                             xn[:, kt * S + tc * 512: kt * S + tc * 512 + 512],
                             start=(kt == 0), stop=(kt == 7))
                C["m1_mm_%d" % i] = sPE.inc(t.last)
            self.t(t_m1)

            def v_m1(v, mt=mt, tc=tc):
                i = tc * 34 + mt
                if i == 0:
                    sDV.wait(v, C["m2_init"])
                sPE.wait(v, C["m1_mm_%d" % i])
                if mt < 16:
                    zi = tc * 16 + mt
                    if zi >= 4:
                        sIn.wait(v, C["m1_zd_%d" % (zi - 4)])
                    v.tensor_copy(zstg[:, (zi % 4) * 512:(zi % 4) * 512 + 512], ps_zx[i % 2][:, :])
                else:
                    v.tensor_copy(xbc[:, (mt - 16) * XBCW + 3: (mt - 16) * XBCW + 3 + 512],
                                  ps_zx[i % 2][:, :])
                C["m1_ev_%d" % i] = sDV.inc(v.last)
            self.v(v_m1)

            if mt < 16:
                def s_m1(sp, mt=mt, tc=tc):
                    i = tc * 34 + mt
                    zi = tc * 16 + mt
                    sDV.wait(sp, C["m1_ev_%d" % i])
                    C["m1_zd_%d" % zi] = sIn.dma(
                        sp.dma_start(out=self.s_z[mt, :, tc * 512:(tc + 1) * 512],
                                     in_=zstg[:, (zi % 4) * 512:(zi % 4) * 512 + 512]))
                self.sp(s_m1)
        C["m1_done_%d" % tc] = C["m1_ev_%d" % (tc * 34 + 33)]
        C["m1_zall_%d" % tc] = C["m1_zd_%d" % (tc * 16 + 15)]

    def _emit_m2_chunk(self, c):
        nc, C = self.nc, self.C
        sW, sIn, sSc, sPE, sDV, sAC = self.sW, self.sIn, self.sSc, self.sPE, self.sDV, self.sAC
        colT, cdb = self.colT, self.cdb
        xbc, zc, thz, uall = self.xbc, self.zc, self.thz, self.uall
        xpost, bct, x_tok, Bmt = self.xpost, self.bct, self.x_tok, self.Bmt
        carry, carry_bf, Gm = self.carry, self.carry_bf, self.Gm
        pre_c, thc = self.pre_c, self.thc
        Lsb, Mt_sb, Mv_sb, Ecsb = self.Lsb, self.Mt_sb, self.Mv_sb, self.Ecsb
        xddb, tbc, sqt, ytmp, csb = self.xddb, self.tbc, self.sqt, self.ytmp, self.csb
        psA, psB, psC, psD = self.psA, self.psB, self.psC, self.psD
        ps_tr = [self.psE, self.psF]
        ps_y = [self.psB, self.psC]
        lc = c % 2

        def s_zl(sp, c=c):
            sIn.wait(sp, C["m1_zall_%d" % (c // 2)])
            if c >= 1:
                sDV.wait(sp, C["m2_u_%d_1" % (c - 1)])
                sAC.wait(sp, C["m2_thz_%d_1" % (c - 1)])
            C["m2_zl_%d" % c] = sIn.dma(
                sp.dma_start(out=zc[:, :].rearrange("p (a b) -> p a b", a=16),
                             in_=self.s_z[:, :, c * CH:(c + 1) * CH]
                             .rearrange("a p b -> p a b")))
        self.sp(s_zl)

        for hf in range(2):
            def a_zs(s, c=c, hf=hf):
                sIn.wait(s, C["m2_zl_%d" % c])
                if hf == 1:
                    sDV.wait(s, C["m2_u_%d_0" % c])
                elif c >= 1:
                    sDV.wait(s, C["m2_u_%d_1" % (c - 1)])
                s.activation(thz[:, :], zc[:, hf * 8 * CH:(hf + 1) * 8 * CH], AF.Tanh, scale=0.5)
                C["m2_thz_%d_%d" % (c, hf)] = sAC.inc(s.last)
            self.s(a_zs)

            def v_zs(v, c=c, hf=hf):
                sAC.wait(v, C["m2_thz_%d_%d" % (c, hf)])
                v.scalar_tensor_tensor(uall[:, hf * 8 * CH:(hf + 1) * 8 * CH], thz[:, :], 1.0,
                                       zc[:, hf * 8 * CH:(hf + 1) * 8 * CH], ALU.add, ALU.mult)
                C["m2_u_%d_%d" % (c, hf)] = sDV.inc(v.last)
            self.v(v_zs)

        for t18 in range(18):
            def v_cv(v, t18=t18, c=c):
                p = t18 % 2
                if t18 == 0:
                    sDV.wait(v, C["m1_done_%d" % (c // 2)])
                    if c >= 1:
                        sPE.wait(v, C["hy_%d_%d" % (c - 1, 31)])
                base = t18 * XBCW + lc * CH
                sl = lambda k: xbc[:, base + k: base + k + CH]
                v.tensor_scalar_mul(pre_c[:, p * CH:(p + 1) * CH], sl(0),
                                    self.convw_sb[:, t18 * 4: t18 * 4 + 1])
                for k in (1, 2, 3):
                    v.scalar_tensor_tensor(pre_c[:, p * CH:(p + 1) * CH], sl(k),
                                           self.convw_sb[:, t18 * 4 + k: t18 * 4 + k + 1],
                                           pre_c[:, p * CH:(p + 1) * CH], ALU.mult, ALU.add)
                C["cv_pre_%d_%d" % (c, t18)] = sDV.inc(v.last)
            self.v(v_cv)

            def a_cv(s, t18=t18, c=c):
                p = t18 % 2
                sDV.wait(s, C["cv_pre_%d_%d" % (c, t18)])
                if t18 >= 2:
                    sDV.wait(s, C["cv_out_%d_%d" % (c, t18 - 2)])
                s.activation(thc[:, p * CH:(p + 1) * CH], pre_c[:, p * CH:(p + 1) * CH],
                             AF.Tanh, scale=0.5, bias=self.convbh[:, t18:t18 + 1])
                C["cv_th_%d_%d" % (c, t18)] = sAC.inc(s.last)
            self.s(a_cv)

            def v_cv2(v, t18=t18, c=c):
                p = t18 % 2
                sAC.wait(v, C["cv_th_%d_%d" % (c, t18)])
                v.tensor_scalar(pre_c[:, p * CH:(p + 1) * CH], pre_c[:, p * CH:(p + 1) * CH],
                                self.convb_sb[:, t18:t18 + 1], 0.5, ALU.add, ALU.mult)
                if t18 < 16:
                    dest = xpost[:, t18 * CH:(t18 + 1) * CH]
                else:
                    dest = bct[:, (t18 - 16) * CH:(t18 - 15) * CH]
                v.scalar_tensor_tensor(dest, thc[:, p * CH:(p + 1) * CH], 1.0,
                                       pre_c[:, p * CH:(p + 1) * CH], ALU.add, ALU.mult)
                C["cv_out_%d_%d" % (c, t18)] = sDV.inc(v.last)
            self.v(v_cv2)
        C["cv_done_%d" % c] = C["cv_out_%d_%d" % (c, 17)]

        for grp in range(4):
            def t_tp(t, grp=grp, c=c):
                st, half = grp // 2, grp % 2
                ps = ps_tr[grp % 2]
                sDV.wait(t, C["cv_out_%d_%d" % (c, (half + 1) * 8 - 1)])
                if grp >= 2:
                    sDV.wait(t, C["tp_ev_%d_%d" % (c, grp - 2)])
                elif c >= 1:
                    if grp == 0:
                        sDV.wait(t, C["bm_ev_%d" % (c - 1)])
                    else:
                        sDV.wait(t, C["tp_ev_%d_%d" % (c - 1, 3)])
                for j in range(8):
                    dt = half * 8 + j
                    t.transpose(ps[:, j * 128:(j + 1) * 128],
                                xpost[:, dt * CH + st * 128: dt * CH + st * 128 + 128], self.idb[:, :])
                C["tp_mm_%d_%d" % (c, grp)] = sPE.inc(t.last)
            self.t(t_tp)

            def v_tp(v, grp=grp, c=c):
                st, half = grp // 2, grp % 2
                ps = ps_tr[grp % 2]
                sPE.wait(v, C["tp_mm_%d_%d" % (c, grp)])
                v.tensor_copy(x_tok[st][:, half * 1024:(half + 1) * 1024], ps[:, :])
                C["tp_ev_%d_%d" % (c, grp)] = sDV.inc(v.last)
            self.v(v_tp)

        def t_bm(t, c=c):
            ps = ps_tr[0]
            sDV.wait(t, C["tp_ev_%d_%d" % (c, 2)])
            sDV.wait(t, C["cv_done_%d" % c])
            for st in range(2):
                t.transpose(ps[:, st * 128:(st + 1) * 128], bct[:, st * 128:(st + 1) * 128], self.idb[:, :])
            C["bm_mm_%d" % c] = sPE.inc(t.last)
        self.t(t_bm)

        def v_bm(v, c=c):
            sPE.wait(v, C["bm_mm_%d" % c])
            v.tensor_copy(Bmt[:, :], ps_tr[0][:, 0:256])
            C["bm_ev_%d" % c] = sDV.inc(v.last)
        self.v(v_bm)

        def t_g(t, c=c):
            sDV.wait(t, C["cv_done_%d" % c])
            if c >= 1:
                sDV.wait(t, C["ms_ev_%d" % (c - 1)])
            t.matmul(psD[:, 0:256], bct[:, 0:128], bct[:, 256:512], start=True, stop=False)
            t.matmul(psD[:, 256:384], bct[:, 128:256], bct[:, 384:512], start=False, stop=True)
            C["g_mm_%d" % c] = sPE.inc(t.last)
        self.t(t_g)

        def v_g(v, c=c):
            sPE.wait(v, C["g_mm_%d" % c])
            v.tensor_mul(Gm[:, 0:128], psD[:, 0:128], self.trif[:, :])
            v.tensor_copy(Gm[:, 128:256], psD[:, 128:256])
            v.tensor_mul(Gm[:, 256:384], psD[:, 256:384], self.trif[:, :])
            C["g_ev_%d" % c] = sDV.inc(v.last)
        self.v(v_g)

        for h in range(32):
            par = h % 2
            pp = h // 2

            def s_csb(s, h=h, c=c, par=par):
                if h == 0 and c == 0:
                    sIn.wait(s, C["p1_csw"])
                if h >= 2:
                    sAC.wait(s, C["hx_%d_%d" % (c, h - 2)])
                    sDV.wait(s, C["hl_%d_%d" % (c, h - 2)])
                elif c >= 1:
                    sAC.wait(s, C["hx_%d_%d" % (c - 1, 30 + h)])
                    sDV.wait(s, C["hl_%d_%d" % (c - 1, 30 + h)])
                C["hcb_%d_%d" % (c, h)] = sSc.dma(
                    s.dma_start(out=csb[:, par * CH:(par + 1) * CH],
                                in_=self.s_cs[h:h + 1, c * CH:(c + 1) * CH].broadcast_to([128, CH])))
            self.s(s_csb)

            def v_lp(v, h=h, c=c, par=par):
                sSc.wait(v, C["hcb_%d_%d" % (c, h)])
                blk0, blk1 = (2 * c) * 96, (2 * c + 1) * 96
                v.tensor_scalar(Lsb[:, par * 384: par * 384 + 256], csb[:, par * CH:(par + 1) * CH],
                                colT[:, blk0 + h: blk0 + h + 1], 0.0, ALU.subtract, ALU.min)
                v.tensor_scalar(Lsb[:, par * 384 + 256: par * 384 + 384],
                                csb[:, par * CH + 128:(par + 1) * CH],
                                colT[:, blk1 + h: blk1 + h + 1], 0.0, ALU.subtract, ALU.min)
                C["hl_%d_%d" % (c, h)] = sDV.inc(v.last)
            self.v(v_lp)

            def a_hx(s, h=h, c=c, par=par):
                sDV.wait(s, C["hl_%d_%d" % (c, h)])
                s.activation(Lsb[:, par * 384:(par + 1) * 384], Lsb[:, par * 384:(par + 1) * 384], AF.Exp)
                s.activation(Ecsb[:, par * CH:(par + 1) * CH], csb[:, par * CH:(par + 1) * CH], AF.Exp)
                C["hx_%d_%d" % (c, h)] = sAC.inc(s.last)
            self.s(a_hx)

            def v_m(v, h=h, c=c, par=par):
                sAC.wait(v, C["hx_%d_%d" % (c, h)])
                if h == 0:
                    sDV.wait(v, C["g_ev_%d" % c])
                    sDV.wait(v, C["tp_ev_%d_3" % c])
                    sDV.wait(v, C["bm_ev_%d" % c])
                blk0, blk1 = (2 * c) * 96, (2 * c + 1) * 96
                v.scalar_tensor_tensor(Mt_sb[:, par * 384: par * 384 + 256],
                                       Lsb[:, par * 384: par * 384 + 256],
                                       colT[:, blk0 + 64 + h: blk0 + 64 + h + 1],
                                       Gm[:, 0:256], ALU.mult, ALU.mult)
                v.scalar_tensor_tensor(Mt_sb[:, par * 384 + 256: par * 384 + 384],
                                       Lsb[:, par * 384 + 256: par * 384 + 384],
                                       colT[:, blk1 + 64 + h: blk1 + 64 + h + 1],
                                       Gm[:, 256:384], ALU.mult, ALU.mult)
                v.tensor_mul(Mv_sb[:, par * CH:(par + 1) * CH], Ecsb[:, par * CH:(par + 1) * CH],
                             bct[:, 256:512])
                v.tensor_scalar_mul(xddb[:, par * 128: par * 128 + 64],
                                    x_tok[0][:, h * 64:(h + 1) * 64],
                                    colT[:, blk0 + 32 + h: blk0 + 32 + h + 1])
                v.tensor_scalar_mul(xddb[:, par * 128 + 64: par * 128 + 128],
                                    x_tok[1][:, h * 64:(h + 1) * 64],
                                    colT[:, blk1 + 32 + h: blk1 + 32 + h + 1])
                C["hm_%d_%d" % (c, h)] = sDV.inc(v.last)
            self.v(v_m)

            if self.debug >= 1 and c == 0 and h in (0, 1):
                def gdbgh(g, h=h, c=c, par=par):
                    sDV.wait(g, C["hm_%d_%d" % (c, h)])
                    self.sW.dma(g.dma_start(out=self.dbg["mt"][par, :, :], in_=Mt_sb[:, par * 384:(par + 1) * 384]))
                    self.sW.dma(g.dma_start(out=self.dbg["mv"][par, :, :], in_=Mv_sb[:, par * CH:(par + 1) * CH]))
                    self.sW.dma(g.dma_start(out=self.dbg["lsb"][par, :, :], in_=Lsb[:, par * 384:(par + 1) * 384]))
                    self.sW.dma(g.dma_start(out=self.dbg["csb"][par, :, :], in_=csb[:, par * CH:(par + 1) * CH]))
                    self.sW.dma(g.dma_start(out=self.dbg["ecs"][par, :, :], in_=Ecsb[:, par * CH:(par + 1) * CH]))
                    if h == 0:
                        self.sW.dma(g.dma_start(out=self.dbg["gm"][:, :], in_=Gm[:, :]))
                        self.sW.dma(g.dma_start(out=self.dbg["bmt"][:, :], in_=Bmt[:, :]))
                        self.sW.dma(g.dma_start(out=self.dbg["colt"][:, :], in_=colT[:, :]))
                        self.sW.dma(g.dma_start(out=self.dbg["xtok"][0, :, :], in_=x_tok[0][:, :]))
                        self.sW.dma(g.dma_start(out=self.dbg["xtok"][1, :, :], in_=x_tok[1][:, :]))
                    C["dbgh_%d" % h] = self.sW.v
                self.g(gdbgh)
                def v_hold(v, h=h):
                    self.sW.wait(v, C["dbgh_%d" % h])
                    v.nop()
                    C["hl_hold_%d" % h] = sDV.inc(v.last)
                self.v(v_hold)

            if par == 1:
                def t_y(t, h=h, c=c, pp=pp):
                    hA, hB = h - 1, h
                    sDV.wait(t, C["hm_%d_%d" % (c, hB)])
                    sDV.wait(t, C["carry_%d_%d" % (c - 1, hA)])
                    sDV.wait(t, C["carry_%d_%d" % (c - 1, hB)])
                    py = ps_y[pp % 2]
                    if pp >= 2:
                        sDV.wait(t, C["tpair_ev_%d_%d" % (c, pp - 2)])
                    elif c >= 1:
                        sDV.wait(t, C["tpair_ev_%d_%d" % (c - 1, pp + 14)])
                    for hi, hh in enumerate((hA, hB)):
                        q = hi % 2
                        ob = hi * 64
                        t.matmul(py[ob:ob + 64, 0:256], x_tok[0][:, hh * 64:(hh + 1) * 64],
                                 Mt_sb[:, q * 384: q * 384 + 256],
                                 start=True, stop=False, tile_position=(0, ob))
                        t.matmul(py[ob:ob + 64, 128:256], x_tok[1][:, hh * 64:(hh + 1) * 64],
                                 Mt_sb[:, q * 384 + 256: q * 384 + 384],
                                 start=False, stop=False, tile_position=(0, ob))
                        t.matmul(py[ob:ob + 64, 0:256], carry_bf[:, hh * 64:(hh + 1) * 64],
                                 Mv_sb[:, q * CH:(q + 1) * CH],
                                 start=False, stop=True, tile_position=(0, ob))
                    if pp % 4 == 0:
                        if pp >= 4:
                            sDV.wait(t, C["carry_%d_%d" % (c, 2 * pp - 1)])
                        elif c >= 1:
                            sDV.wait(t, C["carry_%d_%d" % (c - 1, 31)])
                        if pp == 0:
                            sDV.wait(t, C["m1_ev_%d" % ((c // 2) * 34 + 33)])
                    for hi, hh in enumerate((hA, hB)):
                        q = hi % 2
                        slot = (hh % 8) * 64
                        t.matmul(psA[:, slot:slot + 64], Bmt[:, 0:128], xddb[:, q * 128: q * 128 + 64],
                                 start=(hi == 0), stop=False)
                        t.matmul(psA[:, slot:slot + 64], Bmt[:, 128:256],
                                 xddb[:, q * 128 + 64: q * 128 + 128], start=False, stop=(hi == 1))
                    C["hy_%d_%d" % (c, h)] = sPE.inc(t.last)
                    C["hy_%d_%d" % (c, h - 1)] = C["hy_%d_%d" % (c, h)]
                self.t(t_y)

                def v_cu(v, h=h, c=c):
                    hA, hB = h - 1, h
                    sPE.wait(v, C["hy_%d_%d" % (c, hB)])
                    for hh in (hA, hB):
                        slot = (hh % 8) * 64
                        if hh == hA and c == 0:
                            sAC.wait(v, C["cdb"])
                        v.scalar_tensor_tensor(carry[:, hh * 64:(hh + 1) * 64],
                                               carry[:, hh * 64:(hh + 1) * 64],
                                               cdb[:, hh * 8 + c: hh * 8 + c + 1],
                                               psA[:, slot:slot + 64], ALU.mult, ALU.add)
                        v.tensor_copy(carry_bf[:, hh * 64:(hh + 1) * 64], carry[:, hh * 64:(hh + 1) * 64])
                        C["carry_%d_%d" % (c, hh)] = sDV.inc(v.last)
                self.v(v_cu)

                def v_tp2(v, h=h, c=c, pp=pp):
                    sPE.wait(v, C["hy_%d_%d" % (c, h)])
                    sDV.wait(v, C["m2_u_%d_%d" % (c, pp // 8)])
                    if self.debug and c == 0 and pp == 0:
                        pass
                    py = ps_y[pp % 2]
                    v.scalar_tensor_tensor(ytmp[:, :], xpost[:, pp * CH:(pp + 1) * CH],
                                           self.vecs[:, pp:pp + 1], py[:, 0:256], ALU.mult, ALU.add)
                    v.scalar_tensor_tensor(tbc[:, pp * CH:(pp + 1) * CH], ytmp[:, :], 0.5,
                                           uall[:, pp * CH:(pp + 1) * CH], ALU.mult, ALU.mult)
                    C["tpair_ev_%d_%d" % (c, pp)] = sDV.inc(v.last)
                self.v(v_tp2)

                if self.debug >= 1 and c == 0 and pp == 0:
                    def gdbgy(g, c=c, pp=pp):
                        sDV.wait(g, C["tpair_ev_%d_%d" % (c, pp)])
                        self.sW.dma(g.dma_start(out=self.dbg["ytmp"][:, :], in_=ytmp[:, :]))
                        self.sW.dma(g.dma_start(out=self.dbg["uall"][:, :], in_=uall[:, :]))
                        self.sW.dma(g.dma_start(out=self.dbg["xpost"][:, :], in_=xpost[:, :]))
                        C["dbgy"] = self.sW.v
                    self.g(gdbgy)
                    def v_holdy(v):
                        self.sW.wait(v, C["dbgy"])
                        v.nop()
                        C["holdy"] = sDV.inc(v.last)
                    self.v(v_holdy)

                def a_sqt(s, h=h, c=c, pp=pp):
                    sDV.wait(s, C["tpair_ev_%d_%d" % (c, pp)])
                    if pp >= 2:
                        sPE.wait(s, C["ms_mm_%d_%d" % (c, pp - 2)])
                    s.activation(sqt[:, (pp % 2) * CH:(pp % 2 + 1) * CH],
                                 tbc[:, pp * CH:(pp + 1) * CH], AF.Square)
                    C["sqt_%d_%d" % (c, pp)] = sAC.inc(s.last)
                self.s(a_sqt)

                def t_ms(t, h=h, c=c, pp=pp):
                    sAC.wait(t, C["sqt_%d_%d" % (c, pp)])
                    t.matmul(psD[0:1, 0:256], self.ones_cb[:, :],
                             sqt[:, (pp % 2) * CH:(pp % 2 + 1) * CH],
                             start=(pp == 0), stop=(pp == 15))
                    C["ms_mm_%d_%d" % (c, pp)] = sPE.inc(t.last)
                self.t(t_ms)

        def v_ms(v, c=c):
            sPE.wait(v, C["ms_mm_%d_%d" % (c, 15)])
            if c >= 1:
                sIn.wait(v, C["msw_%d" % (c - 1)])
            v.tensor_scalar_mul(self.msrow[0:1, :], psD[0:1, 0:256], 1.0 / DIN)
            C["ms_ev_%d" % c] = sDV.inc(v.last)
        self.v(v_ms)

        def s_msw(sp, c=c):
            sDV.wait(sp, C["ms_ev_%d" % c])
            C["msw_%d" % c] = sIn.dma(
                sp.dma_start(out=self.s_rows[3:4, c * CH:(c + 1) * CH], in_=self.msrow[0:1, :]))
        self.sp(s_msw)

        def s_td(sp, c=c):
            sDV.wait(sp, C["tpair_ev_%d_%d" % (c, 15)])
            C["m2_tdma_%d" % c] = sIn.dma(
                sp.dma_start(out=self.s_t[:, :, c * CH:(c + 1) * CH]
                             .rearrange("a p b -> p a b"),
                             in_=tbc[:, :].rearrange("p (a b) -> p a b", a=16)))
        self.sp(s_td)

        def v_tguard(v, c=c):
            sIn.wait(v, C["m2_tdma_%d" % c])
            v.nop()
            C["m2_tfree_%d" % c] = sDV.inc(v.last)
        self.v(v_tguard)

        if self.debug >= 1 and c == NCH - 1:
            def gdbg2(g, c=c):
                sDV.wait(g, C["cv_done_%d" % c])
                for t18 in range(16):
                    self.sW.dma(g.dma_start(out=self.dbg["xbc"][t18, :, :],
                                              in_=self.xpost[:, t18 * CH:(t18 + 1) * CH]))
                self.sW.dma(g.dma_start(out=self.dbg["xbc"][16, :, :], in_=self.bct[:, 0:256]))
                self.sW.dma(g.dma_start(out=self.dbg["xbc"][17, :, :], in_=self.bct[:, 256:512]))
            self.g(gdbg2)

    def phase_m3(self, AE):
        nc, C = self.nc, self.C
        sW, sIn, sSc, sPE, sDV, sAC = self.sW, self.sIn, self.sSc, self.sPE, self.sDV, self.sAC
        vecs = self.vecs

        w_out_sb = AE(nc.sbuf_tensor("w_out_sb", [128, 16 * 1024], BF16))
        tld = AE(nc.sbuf_tensor("tld", [128, 2 * 16 * 512], BF16))
        ynb = AE(nc.sbuf_tensor("ynb", [128, 2 * 16 * 512], BF16))
        mstg = AE(nc.sbuf_tensor("mstg", [128, 4 * 512], BF16))
        rmsr = AE(nc.sbuf_tensor("rmsr", [1, S], F32))
        rmsb = AE(nc.sbuf_tensor("rmsb", [128, 2 * 512], F32))
        ps_op = [AE(nc.psum_tensor("ps_opm%d" % i, [128, 512], F32)) for i in range(2)]

        def g_w(g):
            sPE.wait(g, C["m2_pe"])
            if self.debug:
                sW.wait(g, sW.v)
            for kt in range(16):
                sW.dma(g.dma_start(out=w_out_sb[:, kt * 1024:(kt + 1) * 1024],
                                   in_=self.w_outT[kt * 128:(kt + 1) * 128, :]))
            C["w_out_w"] = sW.v
        self.g(g_w)

        def s_msl(sp):
            sIn.wait(sp, C["m2_msw"])
            C["m3_msl"] = sIn.dma(sp.dma_start(out=rmsr[0:1, :], in_=self.s_rows[3:4, :]))
        self.sp(s_msl)

        def a_rms(s):
            sIn.wait(s, C["m3_msl"])
            s.activation(rmsr[0:1, :], rmsr[0:1, :], AF.Ln, bias=self.epsc[0:1, 0:1])
            s.activation(rmsr[0:1, :], rmsr[0:1, :], AF.Exp, scale=-0.5)
            C["m3_rms"] = sAC.inc(s.last)
        self.s(a_rms)

        def s_rmsw(sp):
            sAC.wait(sp, C["m3_rms"])
            C["m3_rmsw"] = sIn.dma(sp.dma_start(out=self.s_rows[4:5, :], in_=rmsr[0:1, :]))
        self.sp(s_rmsw)

        for tc in range(4):
            def s_tl(sp, tc=tc):
                sIn.wait(sp, C["m2_td"])
                sIn.wait(sp, C["m3_rmsw"])
                if tc >= 2:
                    sDV.wait(sp, C["m3_yn_%d" % (tc - 2)])
                C["m3_tl_%d" % tc] = sIn.dma(
                    sp.dma_start(out=tld[:, (tc % 2) * 16 * 512:((tc % 2) + 1) * 16 * 512]
                                 .rearrange("p (a b) -> p a b", a=16),
                                 in_=self.s_t[:, :, tc * 512:(tc + 1) * 512]
                                 .rearrange("a p b -> p a b")))
                C["m3_rb_%d" % tc] = sSc.dma(
                    sp.dma_start(out=rmsb[:, (tc % 2) * 512:(tc % 2) * 512 + 512],
                                 in_=self.s_rows[4:5, tc * 512:(tc + 1) * 512].broadcast_to([128, 512])))
            self.sp(s_tl)

            def v_yn(v, tc=tc):
                sSc.wait(v, C["m3_rb_%d" % tc])
                sIn.wait(v, C["m3_tl_%d" % tc])
                off = (tc % 2) * 16 * 512
                for kt in range(16):
                    v.scalar_tensor_tensor(ynb[:, off + kt * 512: off + (kt + 1) * 512],
                                           tld[:, off + kt * 512: off + (kt + 1) * 512],
                                           vecs[:, 16 + kt:17 + kt],
                                           rmsb[:, (tc % 2) * 512:(tc % 2) * 512 + 512],
                                           ALU.mult, ALU.mult)
                C["m3_yn_%d" % tc] = sDV.inc(v.last)
            self.v(v_yn)

            for mt in range(8):
                def t_op(t, tc=tc, mt=mt):
                    i = tc * 8 + mt
                    if i == 0:
                        sW.wait(t, C["w_out_w"])
                        sDV.wait(t, C["m2_dv"])
                        sAC.wait(t, C["m2_ac"])
                    sDV.wait(t, C["m3_yn_%d" % tc])
                    if i >= 2:
                        sDV.wait(t, C["m3_ev_%d" % (i - 2)])
                    ps = ps_op[i % 2]
                    off = (tc % 2) * 16 * 512
                    for kt in range(16):
                        t.matmul(ps[:, :], w_out_sb[:, kt * 1024 + mt * 128: kt * 1024 + (mt + 1) * 128],
                                 ynb[:, off + kt * 512: off + (kt + 1) * 512],
                                 start=(kt == 0), stop=(kt == 15))
                    C["m3_mm_%d" % i] = sPE.inc(t.last)
                self.t(t_op)

                def v_op(v, tc=tc, mt=mt):
                    i = tc * 8 + mt
                    sPE.wait(v, C["m3_mm_%d" % i])
                    if i >= 4:
                        sIn.wait(v, C["m3_md_%d" % (i - 4)])
                    v.tensor_copy(mstg[:, (i % 4) * 512:(i % 4) * 512 + 512], ps_op[i % 2][:, :])
                    C["m3_ev_%d" % i] = sDV.inc(v.last)
                self.v(v_op)

                def s_op(sp, tc=tc, mt=mt):
                    i = tc * 8 + mt
                    sDV.wait(sp, C["m3_ev_%d" % i])
                    C["m3_md_%d" % i] = sIn.dma(
                        sp.dma_start(out=self.s_m[mt, :, tc * 512:(tc + 1) * 512],
                                     in_=mstg[:, (i % 4) * 512:(i % 4) * 512 + 512]))
                self.sp(s_op)
        C["m3_md"] = C["m3_md_31"]
        C["m3_dv"] = sDV.v
        C["m3_pe"] = sPE.v

        if self.debug >= 1:
            def gdbg3(g):
                sIn.wait(g, C["m3_md"])
                for mt in range(8):
                    self.sW.dma(g.dma_start(out=self.dbg["m"][mt, :, :], in_=self.s_m[mt, :, :]))
                self.sW.dma(g.dma_start(out=self.dbg["rstd"][2:3, :], in_=self.s_rows[3:4, :]))
                self.sW.dma(g.dma_start(out=self.dbg["rstd"][3:4, :], in_=self.s_rows[4:5, :]))
                for kt in range(16):
                    self.sW.dma(g.dma_start(out=self.dbg["t"][kt, :, :], in_=self.s_t[kt, :, :]))
            self.g(gdbg3)

    def phase_attn(self, AE):
        nc, C = self.nc, self.C
        sW, sIn, sSc, sPE, sDV, sAC = self.sW, self.sIn, self.sSc, self.sPE, self.sDV, self.sAC
        xn, vecs = self.xn, self.vecs

        qT = AE(nc.sbuf_tensor("qT", [128, 8 * S], BF16))
        kT = AE(nc.sbuf_tensor("kT", [128, 8 * S], BF16))
        v_store = AE(nc.sbuf_tensor("v_store", [128, 16 * 1056], BF16))
        nbb = AE(nc.sbuf_tensor("nbb", [128, 32], F32))

        with ExitStack() as sub:
            SE = sub.enter_context
            xn2 = SE(nc.sbuf_tensor("xn2", [128, 8 * S], BF16))
            bmax = SE(nc.sbuf_tensor("bmax", [4, 24], F32))
            u2b = SE(nc.sbuf_tensor("u2b", [128, 512], F32))
            r2b = SE(nc.sbuf_tensor("r2b", [128, 2 * 512], F32))
            wv_sb = SE(nc.sbuf_tensor("wv_sb", [128, 8 * 1024], BF16))
            wcol = SE(nc.sbuf_tensor("wcol", [128, 4 * 1024], BF16))
            sqs = SE(nc.sbuf_tensor("sqs", [128, 2 * 512], F32))
            ps_qk = [SE(nc.psum_tensor("ps_qk%d" % i, [128, 512], F32)) for i in range(4)]

            def g_wv(g):
                sPE.wait(g, C["m3_pe"])
                if self.debug:
                    sW.wait(g, sW.v)
                for kt in range(8):
                    sW.dma(g.dma_start(out=wv_sb[:, kt * 1024:(kt + 1) * 1024],
                                       in_=self.wv[kt * 128:(kt + 1) * 128, :]))
                C["wv_w"] = sW.v
            self.g(g_wv)

            for tc in range(4):
                def s_r2(sp, tc=tc):
                    sIn.wait(sp, C["p0_rw2"])
                    if tc >= 2:
                        sDV.wait(sp, C["xn2_%d" % (tc - 2)])
                    C["l2bc_%d" % tc] = sSc.dma(
                        sp.dma_start(out=r2b[:, (tc % 2) * 512:(tc % 2) * 512 + 512],
                                     in_=self.s_rows[2:3, tc * 512:(tc + 1) * 512].broadcast_to([128, 512])))
                self.sp(s_r2)

                def v_l2(v, tc=tc):
                    sSc.wait(v, C["l2bc_%d" % tc])
                    if tc == 0:
                        sDV.wait(v, C["m3_dv"])
                    for kt in range(8):
                        sl = slice(kt * S + tc * 512, kt * S + tc * 512 + 512)
                        v.scalar_tensor_tensor(u2b[:, :], xn[:, sl], vecs[:, 32 + kt:33 + kt],
                                               r2b[:, (tc % 2) * 512:(tc % 2) * 512 + 512],
                                               ALU.mult, ALU.mult)
                        v.tensor_scalar_add(xn2[:, sl], u2b[:, :], vecs[:, 40 + kt:41 + kt])
                    C["xn2_%d" % tc] = sDV.inc(v.last)
                self.v(v_l2)
            C["xn2_all"] = C["xn2_3"]

            if self.debug >= 1:
                def gdbgx2(g):
                    sDV.wait(g, C["xn2_all"])
                    for kt in range(8):
                        self.sW.dma(g.dma_start(out=self.dbg["xn2"][kt, :, :],
                                                  in_=xn2[:, kt * S:(kt + 1) * S]))
                self.g(gdbgx2)

            for pj, wt, dest in [(0, self.wq, qT), (1, self.wk, kT)]:
                for mt in range(8):
                    i = pj * 8 + mt
                    def g_slab(g, i=i, wt=wt, mt=mt):
                        if i >= 4:
                            sPE.wait(g, C["qk_mm_%d_%d" % (i - 4, 3)])
                        C["wqk_%d" % i] = sW.dma(g.dma_start(
                            out=wcol[:, (i % 4) * 1024:(i % 4) * 1024 + 1024]
                                .rearrange("p (kt j) -> p kt j", kt=8),
                            in_=wt[:, mt * 128:(mt + 1) * 128].rearrange("(kt p) j -> p kt j", p=128)))
                    self.g(g_slab)

                    for tc in range(4):
                        def t_qk(t, i=i, mt=mt, tc=tc):
                            gi = i * 4 + tc
                            sW.wait(t, C["wqk_%d" % i])
                            sDV.wait(t, C["xn2_%d" % tc])
                            if gi >= 4:
                                sDV.wait(t, C["qk_ev_%d_%d" % ((gi - 4) // 4, (gi - 4) % 4)])
                            ps = ps_qk[gi % 4]
                            for kt in range(8):
                                t.matmul(ps[:, :],
                                         wcol[:, (i % 4) * 1024 + kt * 128: (i % 4) * 1024 + (kt + 1) * 128],
                                         xn2[:, kt * S + tc * 512: kt * S + tc * 512 + 512],
                                         start=(kt == 0), stop=(kt == 7))
                            C["qk_mm_%d_%d" % (i, tc)] = sPE.inc(t.last)
                        self.t(t_qk)

                        def v_qk(v, i=i, mt=mt, tc=tc, dest=dest):
                            gi = i * 4 + tc
                            sPE.wait(v, C["qk_mm_%d_%d" % (i, tc)])
                            v.tensor_copy(dest[:, mt * S + tc * 512: mt * S + tc * 512 + 512],
                                          ps_qk[gi % 4][:, :])
                            C["qk_ev_%d_%d" % (i, tc)] = sDV.inc(v.last)
                        self.v(v_qk)

            for ttg in range(16):
                def v_vs(v, ttg=ttg):
                    base = v_store[:, ttg * 1056:(ttg + 1) * 1056].rearrange("p (h x) -> p h x", x=33)
                    v.memset(base[:, :, 32:33], 1.0)
                    C["vs1_%d" % ttg] = sDV.inc(v.last)
                self.v(v_vs)
                tc, tt = ttg // 4, ttg % 4
                for half in range(2):
                    gi = ttg * 2 + half
                    def t_v(t, tc=tc, tt=tt, half=half, ttg=ttg, gi=gi):
                        if gi == 0:
                            sW.wait(t, C["wv_w"])
                        sDV.wait(t, C["xn2_%d" % tc])
                        if gi >= 4:
                            sDV.wait(t, C["v_ev_%d" % (gi - 4)])
                        else:
                            sDV.wait(t, C["qk_ev_%d_%d" % (15, gi)])
                        ps = ps_qk[gi % 4]
                        for kt in range(8):
                            t.matmul(ps[:, :],
                                     xn2[:, kt * S + tc * 512 + tt * 128: kt * S + tc * 512 + (tt + 1) * 128],
                                     wv_sb[:, kt * 1024 + half * 512: kt * 1024 + (half + 1) * 512],
                                     start=(kt == 0), stop=(kt == 7))
                        C["v_mm_%d" % gi] = sPE.inc(t.last)
                    self.t(t_v)

                    def v_v(v, half=half, ttg=ttg, gi=gi):
                        sPE.wait(v, C["v_mm_%d" % gi])
                        sDV.wait(v, C["vs1_%d" % ttg])
                        base = v_store[:, ttg * 1056:(ttg + 1) * 1056].rearrange("p (h x) -> p h x", x=33)
                        v.tensor_copy(base[:, half * 16:(half + 1) * 16, 0:32],
                                      ps_qk[gi % 4][:, :].rearrange("p (h x) -> p h x", x=32))
                        C["v_ev_%d" % gi] = sDV.inc(v.last)
                    self.v(v_v)

            def v_bm0(v):
                v.memset(bmax[:, :], 0.0)
                C["bm0"] = sDV.inc(v.last)
            self.v(v_bm0)
            for col, src in [(0, qT), (1, kT)]:
                for dt in range(8):
                    for tc in range(4):
                        sg = (col * 8 + dt) * 4 + tc
                        def a_st(s, col=col, src=src, dt=dt, tc=tc, sg=sg):
                            if sg == 0:
                                sDV.wait(s, C["qk_ev_%d_%d" % (15, 3)])
                                sDV.wait(s, C["v_ev_31"])
                            elif sg >= 2:
                                sPE.wait(s, C["st_mm_%d" % (sg - 2)])
                            s.activation(sqs[:, (sg % 2) * 512:(sg % 2) * 512 + 512],
                                         src[:, dt * S + tc * 512: dt * S + tc * 512 + 512],
                                         AF.Square)
                            C["st_sq_%d" % sg] = sAC.inc(s.last)
                        self.s(a_st)

                        def t_st(t, col=col, dt=dt, tc=tc, sg=sg):
                            sAC.wait(t, C["st_sq_%d" % sg])
                            if sg == 0:
                                sDV.wait(t, C["v_ev_31"])
                            elif sg >= 4:
                                sDV.wait(t, C["st_ev_%d" % (sg - 4)])
                            t.matmul(ps_qk[sg % 4][0:4, 0:512], self.cb4T[:, :],
                                     sqs[:, (sg % 2) * 512:(sg % 2) * 512 + 512],
                                     start=True, stop=True)
                            C["st_mm_%d" % sg] = sPE.inc(t.last)
                        self.t(t_st)

                        def v_st(v, col=col, dt=dt, tc=tc, sg=sg):
                            sPE.wait(v, C["st_mm_%d" % sg])
                            if sg == 0:
                                sDV.wait(v, C["bm0"])
                            v.reduce_max(bmax[0:4, 16:17], ps_qk[sg % 4][0:4, 0:512], AXX)
                            v.tensor_max(bmax[0:4, col * 8 + dt: col * 8 + dt + 1],
                                         bmax[0:4, col * 8 + dt: col * 8 + dt + 1], bmax[0:4, 16:17])
                            C["st_ev_%d" % sg] = sDV.inc(v.last)
                        self.v(v_st)

            def v_b1(v):
                v.tensor_mul(bmax[0:4, 8:16], bmax[0:4, 0:8], bmax[0:4, 8:16])
                C["b_u"] = sDV.inc(v.last)
            self.v(v_b1)

            def a_b(s):
                sDV.wait(s, C["b_u"])
                s.activation(bmax[0:4, 8:16], bmax[0:4, 8:16], AF.Ln)
                s.activation(bmax[0:4, 8:16], bmax[0:4, 8:16], AF.Exp, scale=0.5)
                C["b_sq"] = sAC.inc(s.last)
            self.s(a_b)

            def v_b2(v):
                sAC.wait(v, C["b_sq"])
                v.tensor_scalar_mul(bmax[0:4, 8:16], bmax[0:4, 8:16], -RSQ32)
                C["b_nb"] = sDV.inc(v.last)
            self.v(v_b2)

            def s_nbw(sp):
                sDV.wait(sp, C["b_nb"])
                C["nb_w"] = sIn.dma(sp.dma_start(
                    out=self.s_rows[5:6, 0:32].rearrange("o (dt j) -> (o j) dt", j=4),
                    in_=bmax[0:4, 8:16], allow_slow_non_contiguous=True))
            self.sp(s_nbw)

            def s_nbl(sp):
                sIn.wait(sp, C["nb_w"])
                C["nb_l"] = sSc.dma(sp.dma_start(
                    out=nbb[:, :], in_=self.s_rows[5:6, 0:32].broadcast_to([128, 32])))
            self.sp(s_nbl)
            C["qkv_dv"] = sDV.v

        sub2 = ExitStack()
        attnout = sub2.enter_context(nc.sbuf_tensor("attnout", [128, 8 * S], BF16))
        with ExitStack() as sub:
            SE = sub.enter_context
            pstg = SE(nc.sbuf_tensor("pstg", [128, 4 * 1024], BF16))
            sstg = SE(nc.sbuf_tensor("sstg", [33, 4 * 1024], F32))
            pT = SE(nc.sbuf_tensor("pT", [128, 4 * 1024], BF16))
            ps_s = [SE(nc.psum_tensor("ps_s%d" % i, [128, 1024], F32)) for i in range(2)]
            ps_pv = [SE(nc.psum_tensor("ps_pv%d" % i, [33, 1024], F32)) for i in range(2)]

            for pp in range(16):
                dt, u = pp // 2, pp % 2
                hA, hB = 4 * dt + 2 * u, 4 * dt + 2 * u + 1
                for qc in range(2):
                    gi = pp * 2 + qc
                    for kt in range(16):
                        def t_sv(t, dt=dt, u=u, hA=hA, hB=hB, qc=qc, kt=kt, gi=gi):
                            if gi == 0 and kt == 0:
                                sDV.wait(t, C["qkv_dv"])
                                sSc.wait(t, C["nb_l"])
                            if kt > 0:
                                sAC.wait(t, C["f_x_%d_%d" % (gi, kt - 1)])
                            elif gi > 0:
                                sAC.wait(t, C["f_x_%d_%d" % (gi - 1, 15)])
                            for hi in range(2):
                                strip = 2 * u + hi
                                for nck in range(2):
                                    t.matmul(ps_s[hi][:, nck * 512:(nck + 1) * 512],
                                             kT[32 * strip:32 * strip + 32,
                                                dt * S + kt * 128: dt * S + (kt + 1) * 128],
                                             qT[32 * strip:32 * strip + 32,
                                                dt * S + qc * 1024 + nck * 512:
                                                dt * S + qc * 1024 + (nck + 1) * 512],
                                             start=True, stop=True, tile_position=(32 * strip, 0))
                            C["f_s_%d_%d" % (gi, kt)] = sPE.inc(t.last)
                            if kt > 0:
                                if kt - 1 == 0 and gi >= 1:
                                    sDV.wait(t, C["f_ev_%d" % (gi - 1)])
                                for hi, h in enumerate((hA, hB)):
                                    for nck in range(2):
                                        t.matmul(ps_pv[hi][:, nck * 512:(nck + 1) * 512],
                                                 v_store[:, (kt - 1) * 1056 + 33 * h:
                                                         (kt - 1) * 1056 + 33 * h + 33],
                                                 pT[:, (hi * 2 + ((kt - 1) % 2)) * 1024 + nck * 512:
                                                    (hi * 2 + ((kt - 1) % 2)) * 1024 + (nck + 1) * 512],
                                                 start=(kt - 1 == 0), stop=False)
                                C["f_pv_%d_%d" % (gi, kt - 1)] = sPE.inc(t.last)
                        self.t(t_sv)

                        def a_x(s, hA=hA, hB=hB, qc=qc, kt=kt, gi=gi):
                            sPE.wait(s, C["f_s_%d_%d" % (gi, kt)])
                            if kt >= 2:
                                sPE.wait(s, C["f_pv_%d_%d" % (gi, kt - 2)])
                            elif gi >= 1:
                                if kt == 0:
                                    sPE.wait(s, C["f_pv_%d_%d" % (gi - 1, 14)])
                                else:
                                    sPE.wait(s, C["f_pvl_%d" % (gi - 1)])
                            for hi, h in enumerate((hA, hB)):
                                s.activation(pT[:, (hi * 2 + kt % 2) * 1024:(hi * 2 + kt % 2 + 1) * 1024],
                                             ps_s[hi][:, :], AF.Exp, scale=RSQ32, bias=nbb[:, h:h + 1])
                            C["f_x_%d_%d" % (gi, kt)] = sAC.inc(s.last)
                        self.s(a_x)

                    def t_pvl(t, hA=hA, hB=hB, gi=gi):
                        sAC.wait(t, C["f_x_%d_%d" % (gi, 15)])
                        for hi, h in enumerate((hA, hB)):
                            for nck in range(2):
                                t.matmul(ps_pv[hi][:, nck * 512:(nck + 1) * 512],
                                         v_store[:, 15 * 1056 + 33 * h: 15 * 1056 + 33 * h + 33],
                                         pT[:, (hi * 2 + 1) * 1024 + nck * 512:
                                            (hi * 2 + 1) * 1024 + (nck + 1) * 512],
                                         start=False, stop=True)
                        C["f_pvl_%d" % gi] = sPE.inc(t.last)
                    self.t(t_pvl)

                    def v_ev(v, dt=dt, u=u, hA=hA, hB=hB, qc=qc, gi=gi):
                        sPE.wait(v, C["f_pvl_%d" % gi])
                        if gi >= 2:
                            sSc.wait(v, C["f_evd_%d" % (gi - 2)])
                        for hi, h in enumerate((hA, hB)):
                            sl = ((gi % 2) * 2 + hi) * 1024
                            v.tensor_copy(pstg[0:32, sl:sl + 1024], ps_pv[hi][0:32, :])
                            v.tensor_copy(sstg[32:33, sl:sl + 1024], ps_pv[hi][32:33, :])
                        C["f_ev_%d" % gi] = sDV.inc(v.last)
                    self.v(v_ev)

                    def s_ev(sp, dt=dt, u=u, hA=hA, hB=hB, qc=qc, gi=gi):
                        sDV.wait(sp, C["f_ev_%d" % gi])
                        for hi, h in enumerate((hA, hB)):
                            strip = 2 * u + hi
                            sl = ((gi % 2) * 2 + hi) * 1024
                            sSc.dma(sp.dma_start(
                                out=attnout[32 * strip:32 * strip + 32,
                                            dt * S + qc * 1024: dt * S + (qc + 1) * 1024],
                                in_=pstg[0:32, sl:sl + 1024]))
                            sSc.dma(sp.dma_start(
                                out=self.s_sums[h:h + 1, qc * 1024:(qc + 1) * 1024],
                                in_=sstg[32:33, sl:sl + 1024]))
                        C["f_evd_%d" % gi] = sSc.v
                    self.s(s_ev)
            C["f_attn_dv"] = sDV.v
            C["f_attn_dma"] = sSc.v

        sub3 = ExitStack()
        sums_sb = sub3.enter_context(nc.sbuf_tensor("sums_sb", [32, S], F32))
        ostg = sub3.enter_context(nc.sbuf_tensor("ostg", [128, 4 * 512], BF16))

        def s_rsl(sp):
            sSc.wait(sp, C["f_evd_31"])
            C["f_rsl"] = sIn.dma(sp.dma_start(out=sums_sb[:, :], in_=self.s_sums[:, :]))
        self.sp(s_rsl)

        def v_rc(v):
            sIn.wait(v, C["f_rsl"])
            v.reciprocal(sums_sb[:, :], sums_sb[:, :])
            C["f_rs"] = sDV.inc(v.last)
        self.v(v_rc)

        def s_rsw(sp):
            sDV.wait(sp, C["f_rs"])
            C["f_rsw"] = sIn.dma(sp.dma_start(out=self.s_sums[:, :], in_=sums_sb[:, :]))
        self.sp(s_rsw)

        with ExitStack() as sub:
            SE = sub.enter_context
            wcol2 = SE(nc.sbuf_tensor("wcol2", [128, 2 * 1024], BF16))
            rbb = SE(nc.sbuf_tensor("rbb", [128, 2 * 512], F32))
            ps_o = [SE(nc.psum_tensor("ps_o%d" % i, [128, 512], F32)) for i in range(2)]

            for dt in range(8):
                for tc in range(4):
                    gi = dt * 4 + tc
                    def s_rb(sp, dt=dt, tc=tc, gi=gi):
                        sIn.wait(sp, C["f_rsw"])
                        if gi >= 2:
                            sDV.wait(sp, C["f_nm_%d" % (gi - 2)])
                        for h in range(4):
                            C["f_rb_%d" % gi] = sSc.dma(sp.dma_start(
                                out=rbb[h * 32:(h + 1) * 32, (gi % 2) * 512:(gi % 2) * 512 + 512],
                                in_=self.s_sums[4 * dt + h:4 * dt + h + 1, tc * 512:(tc + 1) * 512]
                                    .broadcast_to([32, 512])))
                    self.sp(s_rb)

                    def v_nm(v, dt=dt, tc=tc, gi=gi):
                        sSc.wait(v, C["f_rb_%d" % gi])
                        sl = slice(dt * S + tc * 512, dt * S + tc * 512 + 512)
                        v.tensor_mul(attnout[:, sl], attnout[:, sl],
                                     rbb[:, (gi % 2) * 512:(gi % 2) * 512 + 512])
                        C["f_nm_%d" % gi] = sDV.inc(v.last)
                    self.v(v_nm)
            C["f_nm"] = sDV.v

            if self.debug >= 1:
                def gdbga(g):
                    sDV.wait(g, C["f_nm"])
                    for dt in range(8):
                        self.sW.dma(g.dma_start(out=self.dbg["ao"][dt, :, :],
                                                  in_=attnout[:, dt * S:(dt + 1) * S]))
                self.g(gdbga)

            for mt in range(8):
                def g_wo(g, mt=mt):
                    if mt >= 2:
                        sPE.wait(g, C["o_mm_%d_%d" % (mt - 2, 3)])
                    else:
                        sPE.wait(g, C["f_pvl_31"])
                    C["wo_%d" % mt] = sW.dma(g.dma_start(
                        out=wcol2[:, (mt % 2) * 1024:(mt % 2) * 1024 + 1024]
                            .rearrange("p (kt j) -> p kt j", kt=8),
                        in_=self.wo[:, mt * 128:(mt + 1) * 128].rearrange("(kt p) j -> p kt j", p=128)))
                self.g(g_wo)

                for tc in range(4):
                    gi = mt * 4 + tc
                    def t_o(t, mt=mt, tc=tc, gi=gi):
                        sW.wait(t, C["wo_%d" % mt])
                        if gi == 0:
                            sDV.wait(t, C["f_nm"])
                        if gi >= 2:
                            sDV.wait(t, C["o_ev_%d" % (gi - 2)])
                        ps = ps_o[gi % 2]
                        for kt in range(8):
                            t.matmul(ps[:, :],
                                     wcol2[:, (mt % 2) * 1024 + kt * 128: (mt % 2) * 1024 + (kt + 1) * 128],
                                     attnout[:, kt * S + tc * 512: kt * S + tc * 512 + 512],
                                     start=(kt == 0), stop=(kt == 7))
                        C["o_mm_%d_%d" % (mt, tc)] = sPE.inc(t.last)
                    self.t(t_o)

                    def v_o(v, mt=mt, tc=tc, gi=gi):
                        sPE.wait(v, C["o_mm_%d_%d" % (mt, tc)])
                        if gi >= 4:
                            sIn.wait(v, C["o_dma_%d" % (gi - 4)])
                        v.tensor_add(ostg[:, (gi % 4) * 512:(gi % 4) * 512 + 512],
                                     ps_o[gi % 2][:, :], xn[:, mt * S + tc * 512: mt * S + tc * 512 + 512])
                        C["o_ev_%d" % gi] = sDV.inc(v.last)
                    self.v(v_o)

                    def s_so(sp, mt=mt, tc=tc, gi=gi):
                        sDV.wait(sp, C["o_ev_%d" % gi])
                        C["o_dma_%d" % gi] = sIn.dma(
                            sp.dma_start(out=self.s_o[mt, :, tc * 512:(tc + 1) * 512],
                                         in_=ostg[:, (gi % 4) * 512:(gi % 4) * 512 + 512]))
                    self.sp(s_so)
            C["o_done"] = C["o_dma_31"]
            C["f_dv"] = sDV.v
            C["f_pe"] = sPE.v

            if self.debug >= 1:
                def gdbgo(g):
                    sIn.wait(g, C["o_done"])
                    for mt in range(8):
                        self.sW.dma(g.dma_start(out=self.dbg["o"][mt, :, :], in_=self.s_o[mt, :, :]))
                self.g(gdbgo)
        sub3.close()
        sub2.close()

    def phase_moe_p5(self, AE):
        nc, C = self.nc, self.C
        sW, sIn, sPE, sDV, sAC, sOut = self.sW, self.sIn, self.sPE, self.sDV, self.sAC, self.sOut

        osl = AE(nc.sbuf_tensor("osl", [128, 8 * 1024], BF16))
        w1s = AE(nc.sbuf_tensor("w1s", [128, 8 * 2048], BF16))
        w2s = AE(nc.sbuf_tensor("w2s", [128, 16 * 1024], BF16))
        h_sb = AE(nc.sbuf_tensor("h_sb", [128, 16 * 1024], BF16))
        y_acc = AE(nc.sbuf_tensor("y_acc", [128, 8 * 1024], F32))
        xstg = AE(nc.sbuf_tensor("xstg", [128, 2 * 512], F32))
        mstg2 = AE(nc.sbuf_tensor("mstg2", [128, 2 * 512], BF16))
        fstg = AE(nc.sbuf_tensor("fstg", [128, 2 * 512], F32))
        ps_h = [AE(nc.psum_tensor("ps_h%d" % i, [128, 512], F32)) for i in range(2)]
        ps_yy = [AE(nc.psum_tensor("ps_yy%d" % i, [128, 512], F32)) for i in range(2)]

        for e in range(2):
            tok0 = e * 1024

            def s_ol(sp, e=e, tok0=tok0):
                sIn.wait(sp, C["o_done"])
                if e == 0:
                    sPE.wait(sp, C["f_pe"])
                else:
                    sPE.wait(sp, C["moe_hlast_%d_%d" % (0, 1)])
                C["osl_%d" % e] = sIn.dma(
                    sp.dma_start(out=osl[:, :].rearrange("p (a b) -> p a b", a=8),
                                 in_=self.s_o[:, :, tok0:tok0 + 1024]
                                 .rearrange("a p b -> p a b")))
            self.sp(s_ol)

            for fh in range(2):
                def g_w1(g, e=e, fh=fh):
                    if e == 0 and fh == 0:
                        sPE.wait(g, C["f_pe"])
                        if self.debug:
                            sW.wait(g, sW.v)
                    elif fh == 1:
                        sPE.wait(g, C["moe_hlast_%d_%d" % (e, 0)])
                    else:
                        sPE.wait(g, C["moe_hlast_%d_%d" % (e - 1, 1)])
                    for kt in range(8):
                        sW.dma(g.dma_start(out=w1s[:, kt * 2048:(kt + 1) * 2048],
                                           in_=self.w1[e, kt * 128:(kt + 1) * 128,
                                                       fh * 2048:(fh + 1) * 2048]))
                    C["w1_%d_%d" % (e, fh)] = sW.v
                self.g(g_w1)

                def g_w2(g, e=e, fh=fh):
                    if e == 0 and fh == 0:
                        sPE.wait(g, C["f_pe"])
                    elif fh == 1:
                        sPE.wait(g, C["moe_ylast_%d_%d" % (e, 0)])
                    else:
                        sPE.wait(g, C["moe_ylast_%d_%d" % (e - 1, 1)])
                    for ft in range(16):
                        sW.dma(g.dma_start(out=w2s[:, ft * 1024:(ft + 1) * 1024],
                                           in_=self.w2[e, fh * 2048 + ft * 128:
                                                       fh * 2048 + (ft + 1) * 128, :]))
                    C["w2_%d_%d" % (e, fh)] = sW.v
                self.g(g_w2)

                for ft in range(16):
                    for tt in range(2):
                        gi = ft * 2 + tt

                        def t_h(t, e=e, fh=fh, ft=ft, tt=tt, gi=gi):
                            if gi == 0:
                                sW.wait(t, C["w1_%d_%d" % (e, fh)])
                                sIn.wait(t, C["osl_%d" % e])
                            if gi >= 2:
                                sAC.wait(t, C["moe_g_%d_%d_%d" % (e, fh, gi - 2)])
                            ps = ps_h[gi % 2]
                            for kt in range(8):
                                t.matmul(ps[:, :],
                                         w1s[:, kt * 2048 + ft * 128: kt * 2048 + (ft + 1) * 128],
                                         osl[:, kt * 1024 + tt * 512: kt * 1024 + (tt + 1) * 512],
                                         start=(kt == 0), stop=(kt == 7))
                            C["moe_h_%d_%d_%d" % (e, fh, gi)] = sPE.inc(t.last)
                        self.t(t_h)

                        def a_g(s, e=e, fh=fh, ft=ft, tt=tt, gi=gi):
                            sPE.wait(s, C["moe_h_%d_%d_%d" % (e, fh, gi)])
                            s.activation(h_sb[:, ft * 1024 + tt * 512: ft * 1024 + (tt + 1) * 512],
                                         ps_h[gi % 2][:, :], AF.Gelu,
                                         bias=self.b1_sb[:, e * 32 + fh * 16 + ft:
                                                         e * 32 + fh * 16 + ft + 1])
                            C["moe_g_%d_%d_%d" % (e, fh, gi)] = sAC.inc(s.last)
                        self.s(a_g)
                C["moe_hlast_%d_%d" % (e, fh)] = C["moe_h_%d_%d_%d" % (e, fh, 31)]

                for dt in range(8):
                    for tt in range(2):
                        gi = dt * 2 + tt

                        def t_yy(t, e=e, fh=fh, dt=dt, tt=tt, gi=gi):
                            if gi == 0:
                                sW.wait(t, C["w2_%d_%d" % (e, fh)])
                            sAC.wait(t, C["moe_g_%d_%d_%d" % (e, fh, 31)])
                            if gi >= 2:
                                sDV.wait(t, C["moe_ye_%d_%d_%d" % (e, fh, gi - 2)])
                            ps = ps_yy[gi % 2]
                            for ft in range(16):
                                t.matmul(ps[:, :],
                                         w2s[:, ft * 1024 + dt * 128: ft * 1024 + (dt + 1) * 128],
                                         h_sb[:, ft * 1024 + tt * 512: ft * 1024 + (tt + 1) * 512],
                                         start=(ft == 0), stop=(ft == 15))
                            C["moe_y_%d_%d_%d" % (e, fh, gi)] = sPE.inc(t.last)
                        self.t(t_yy)

                        def v_yy(v, e=e, fh=fh, dt=dt, tt=tt, gi=gi):
                            sPE.wait(v, C["moe_y_%d_%d_%d" % (e, fh, gi)])
                            sl = slice(dt * 1024 + tt * 512, dt * 1024 + (tt + 1) * 512)
                            if fh == 0:
                                v.tensor_copy(y_acc[:, sl], ps_yy[gi % 2][:, :])
                            else:
                                v.tensor_add(y_acc[:, sl], y_acc[:, sl], ps_yy[gi % 2][:, :])
                            C["moe_ye_%d_%d_%d" % (e, fh, gi)] = sDV.inc(v.last)
                        self.v(v_yy)
                C["moe_ylast_%d_%d" % (e, fh)] = C["moe_y_%d_%d_%d" % (e, fh, 15)]

            for dt in range(8):
                for tt in range(2):
                    gi = dt * 2 + tt

                    def s_xl(sp, e=e, dt=dt, tt=tt, gi=gi, tok0=tok0):
                        if e > 0 or gi >= 2:
                            key = "p5_ev_%d_%d" % ((e, gi - 2) if gi >= 2 else (e - 1, 14 + gi))
                            sDV.wait(sp, C[key])
                        sl = slice(tok0 + tt * 512, tok0 + (tt + 1) * 512)
                        C["p5_x_%d_%d" % (e, gi)] = sIn.dma(
                            sp.dma_start(out=xstg[:, (gi % 2) * 512:(gi % 2) * 512 + 512],
                                         in_=self.xT[dt * 128:(dt + 1) * 128, sl]))
                        C["p5_m_%d_%d" % (e, gi)] = sIn.dma(
                            sp.dma_start(out=mstg2[:, (gi % 2) * 512:(gi % 2) * 512 + 512],
                                         in_=self.s_m[dt, :, sl]))
                    self.sp(s_xl)

                    def v_p5(v, e=e, dt=dt, tt=tt, gi=gi):
                        sIn.wait(v, C["p5_m_%d_%d" % (e, gi)])
                        sDV.wait(v, C["moe_ye_%d_%d_%d" % (e, 1, gi)])
                        if e > 0 or gi >= 2:
                            key = "p5_od_%d_%d" % ((e, gi - 2) if gi >= 2 else (e - 1, 14 + gi))
                            sOut.wait(v, C[key])
                        sl = slice(dt * 1024 + tt * 512, dt * 1024 + (tt + 1) * 512)
                        fs = fstg[:, (gi % 2) * 512:(gi % 2) * 512 + 512]
                        v.tensor_scalar_add(fs, y_acc[:, sl], self.b2_sb[:, e * 8 + dt: e * 8 + dt + 1])
                        v.tensor_add(fs, fs, mstg2[:, (gi % 2) * 512:(gi % 2) * 512 + 512])
                        v.tensor_add(fs, fs, xstg[:, (gi % 2) * 512:(gi % 2) * 512 + 512])
                        C["p5_ev_%d_%d" % (e, gi)] = sDV.inc(v.last)
                    self.v(v_p5)

                    def s_out(sp, e=e, dt=dt, tt=tt, gi=gi, tok0=tok0):
                        sDV.wait(sp, C["p5_ev_%d_%d" % (e, gi)])
                        sl = slice(tok0 + tt * 512, tok0 + (tt + 1) * 512)
                        C["p5_od_%d_%d" % (e, gi)] = sOut.dma(
                            sp.dma_start(out=self.outT[dt * 128:(dt + 1) * 128, sl],
                                         in_=fstg[:, (gi % 2) * 512:(gi % 2) * 512 + 512]))
                    self.sp(s_out)


def _mk_consts():
    cid = np.eye(128, dtype=np.float32)
    ctri = (np.arange(128)[None, :] >= np.arange(128)[:, None]).astype(np.float32)
    return cid, ctri


def _core_inputs(core, inp):
    b = core
    e0 = 2 * (b % 4)
    cid, ctri = _mk_consts()
    A = np.ascontiguousarray
    d = {
        "xT": A(np.asarray(inp["L0_attn_inputs"])[b].T),
        "w_inT": A(np.asarray(inp["L0_attn_W_in"]).T),
        "convw": A(np.asarray(inp["L0_attn_conv_w"]).reshape(18, 128, 4).transpose(1, 0, 2).reshape(128, 72)),
        "convb": A(np.asarray(inp["L0_attn_conv_b"]).reshape(18, 128).T),
        "dtbias": A(np.asarray(inp["L0_attn_dt_bias"]).reshape(32, 1)),
        "alog": A(np.asarray(inp["L0_attn_A_log"]).reshape(32, 1)),
        "drepd": A(np.repeat(np.asarray(inp["L0_attn_D"]), 64).reshape(16, 128).T),
        "normwd": A(np.asarray(inp["L0_attn_norm_w"]).reshape(16, 128).T),
        "w_outT": A(np.asarray(inp["L0_attn_W_out"]).T),
        "lngd": A(np.asarray(inp["L0_ffn_ln_g"]).reshape(8, 128).T),
        "lnbd": A(np.asarray(inp["L0_ffn_ln_b"]).reshape(8, 128).T),
        "wq": A(inp["L0_ffn_wq"]), "wk": A(inp["L0_ffn_wk"]),
        "wv": A(inp["L0_ffn_wv"]), "wo": A(inp["L0_ffn_wo"]),
        "w1": A(np.asarray(inp["L0_ffn_w1"])[e0:e0 + 2]),
        "w2": A(np.asarray(inp["L0_ffn_w2"])[e0:e0 + 2]),
        "b1t": A(np.asarray(inp["L0_ffn_b1"])[e0:e0 + 2].reshape(2, 32, 128).transpose(0, 2, 1)),
        "b2t": A(np.asarray(inp["L0_ffn_b2"])[e0:e0 + 2].reshape(2, 8, 128).transpose(0, 2, 1)),
        "cid": cid, "ctri": ctri,
    }
    return {k: np.ascontiguousarray(np.asarray(v, dtype=np.float32)) for k, v in d.items()}


_prog_cache = {}


def _get_prog(debug=0, upto="all"):
    key = (debug, upto)
    if key not in _prog_cache:
        _prog_cache[key] = Builder(debug, upto).build()
    return _prog_cache[key]


def kernel(**inputs):
    nc, _ = _get_prog(_DEBUG, _UPTO)
    in_maps = [_core_inputs(b, inputs) for b in range(8)]
    res = run_bass_kernel_spmd(nc, in_maps, core_ids=list(range(8)))
    out = np.stack([np.ascontiguousarray(res.results[b]["outT"].T) for b in range(8)])
    return out.astype(np.float32)


# version 02
# speedup vs baseline: 1.2851x; 1.2851x over previous
import numpy as np
from contextlib import ExitStack

import concourse.bass as bass
import concourse.mybir as mybir
from concourse.bass_utils import run_bass_kernel_spmd

F32 = mybir.dt.float32
BF16 = mybir.dt.bfloat16
AF = mybir.ActivationFunctionType
ALU = mybir.AluOpType
AXX = mybir.AxisListType.X

S = 2048
D = 1024
DIN = 2048
DPROJ = 4384
CH = 256
NCH = S // CH
FD = 4096
EPS = 1e-5
RSQ32 = float(1.0 / np.sqrt(np.float32(32.0)))
XBCW = 512 + 3

_DEBUG = 0
_UPTO = "all"


class Sem:
    def __init__(self, nc, ctx, name):
        self.h = ctx.enter_context(nc.semaphore(name))
        self.v = 0

    def inc(self, inst, k=1):
        self.v += k
        inst.then_inc(self.h, k)
        return self.v

    def dma(self, inst):
        return self.inc(inst, 16)

    def wait(self, eng, v):
        if v > 0:
            eng.wait_ge(self.h, v)



class EP:

    def __init__(self, eng):
        object.__setattr__(self, "_e", eng)
        object.__setattr__(self, "last", None)

    def __getattr__(self, n):
        f = getattr(self._e, n)
        if not callable(f):
            return f

        def w(*a, **k):
            r = f(*a, **k)
            if n != "wait_ge" and r is not None:
                object.__setattr__(self, "last", r)
            return r
        return w


class Builder:
    def __init__(self, debug=0, upto="all"):
        self.debug = debug
        self.upto = upto
        self.C = {}

    def t(self, fn): self.blk.tensor(lambda e: fn(EP(e)))
    def v(self, fn): self.blk.vector(lambda e: fn(EP(e)))
    def s(self, fn): self.blk.scalar(lambda e: fn(EP(e)))
    def g(self, fn): self.blk.gpsimd(lambda e: fn(EP(e)))
    def sp(self, fn): self.blk.sync(lambda e: fn(EP(e)))

    def build(self):
        nc = bass.Bass("TRN2", target_bir_lowering=False, detect_race_conditions=False)
        self.nc = nc
        C = self.C

        dram = lambda n, sh, dd=F32, kind="ExternalInput": nc.dram_tensor(n, sh, dd, kind=kind).ap()
        self.xT = dram("xT", [D, S])
        self.w_inT = dram("w_inT", [D, DPROJ])
        self.convw = dram("convw", [128, 72])
        self.convb = dram("convb", [128, 18])
        self.dtbias = dram("dtbias", [32, 1])
        self.alog = dram("alog", [32, 1])
        self.drepd = dram("drepd", [128, 16])
        self.normwd = dram("normwd", [128, 16])
        self.w_outT = dram("w_outT", [DIN, D])
        self.lngd = dram("lngd", [128, 8])
        self.lnbd = dram("lnbd", [128, 8])
        self.wq = dram("wq", [D, D]); self.wk = dram("wk", [D, D])
        self.wv = dram("wv", [D, D]); self.wo = dram("wo", [D, D])
        self.w1 = dram("w1", [2, D, FD]); self.w2 = dram("w2", [2, FD, D])
        self.b1t = dram("b1t", [2, 128, 32]); self.b2t = dram("b2t", [2, 128, 8])
        self.cid = dram("cid", [128, 128])
        self.ctri = dram("ctri", [128, 128])
        self.outT = dram("outT", [D, S], F32, "ExternalOutput")
        self.s_z = dram("s_z", [16, 128, S], BF16, "Internal")
        self.s_t = dram("s_t", [16, 128, S], BF16, "Internal")
        self.s_m = dram("s_m", [8, 128, S], BF16, "Internal")
        self.s_o = dram("s_o", [8, 128, S], BF16, "Internal")
        self.s_rows = dram("s_rows", [8, S], F32, "Internal")
        self.s_cs = dram("s_cs", [32, S], F32, "Internal")
        self.s_sums = dram("s_sums", [32, S], F32, "Internal")

        self.dbg = {}
        if self.debug >= 1:
            def dbgt(name, sh):
                self.dbg[name] = dram("dbg_" + name, sh, F32, "ExternalOutput")
            dbgt("xn", [8, 128, S])
            dbgt("dtp", [32, S])
            dbgt("cs", [32, S])
            dbgt("dd", [32, S])
            dbgt("xbc", [18, 128, 256])
            dbgt("t", [16, 128, S])
            dbgt("m", [8, 128, S])
            dbgt("rstd", [5, S])
            dbgt("xn2", [8, 128, S])
            dbgt("ao", [8, 128, S])
            dbgt("o", [8, 128, S])
            dbgt("mt", [2, 128, 384])
            dbgt("mv", [2, 128, 256])
            dbgt("lsb", [2, 128, 384])
            dbgt("csb", [2, 128, 256])
            dbgt("xtok", [2, 128, 2048])
            dbgt("gm", [128, 384])
            dbgt("bmt", [128, 256])
            dbgt("colt", [128, 16 * 96])
            dbgt("ecs", [2, 128, 256])
            dbgt("ytmp", [128, 256])
            dbgt("uall", [128, 4096])
            dbgt("xpost", [128, 4096])

        with ExitStack() as top:
            TE = top.enter_context
            self.colT = TE(nc.sbuf_tensor("colT", [128, 16 * 96], F32))
            self.ones_c = TE(nc.sbuf_tensor("ones_c", [128, 1], F32))
            self.ones_cb = TE(nc.sbuf_tensor("ones_cb", [128, 1], BF16))
            self.epsc = TE(nc.sbuf_tensor("epsc", [128, 1], F32))
            self.cb4T = TE(nc.sbuf_tensor("cb4T", [128, 4], F32))
            self.idb = TE(nc.sbuf_tensor("idb", [128, 128], BF16))
            self.idf = TE(nc.sbuf_tensor("idf", [128, 128], F32))
            self.trif = TE(nc.sbuf_tensor("trif", [128, 128], F32))
            self.vecs = TE(nc.sbuf_tensor("vecs", [128, 64], F32))
            self.convb_sb = TE(nc.sbuf_tensor("convb_sb", [128, 18], F32))
            self.convbh = TE(nc.sbuf_tensor("convbh", [128, 18], F32))
            self.convw_sb = TE(nc.sbuf_tensor("convw_sb", [128, 72], F32))
            self.smv = TE(nc.sbuf_tensor("smv", [32, 8], F32))
            self.b1_sb = TE(nc.sbuf_tensor("b1_sb", [128, 64], F32))
            self.b2_sb = TE(nc.sbuf_tensor("b2_sb", [128, 16], F32))

            self.sW = Sem(nc, top, "sW")
            self.sIn = Sem(nc, top, "sIn")
            self.sSc = Sem(nc, top, "sSc")
            self.sPE = Sem(nc, top, "sPE")
            self.sDV = Sem(nc, top, "sDV")
            self.sAC = Sem(nc, top, "sAC")
            self.sOut = Sem(nc, top, "sOut")

            self.blk = TE(nc.Block())
            self.xn = TE(nc.sbuf_tensor("xn", [128, 8 * S], BF16))

            with ExitStack() as scA:
                self.phase_p0(scA.enter_context)
                self.phase_p1(scA.enter_context)
                if self.upto == "p1":
                    self.finish_dummy()
                    return nc, self.dbg
                self.phase_m1_m2(scA.enter_context)
            with ExitStack() as scB:
                self.phase_m3(scB.enter_context)
            if self.upto == "m":
                self.finish_dummy()
                return nc, self.dbg
            with ExitStack() as scC:
                self.phase_attn(scC.enter_context)
            if self.upto == "f":
                self.finish_dummy()
                return nc, self.dbg
            with ExitStack() as scD:
                self.phase_moe_p5(scD.enter_context)

            def fin(sp):
                self.sOut.wait(sp, self.sOut.v)
            self.sp(fin)
        return nc, self.dbg

    def finish_dummy(self):
        nc = self.nc
        with ExitStack() as sc:
            z = sc.enter_context(nc.sbuf_tensor("zfin", [128, 512], F32))

            def v_fin(v):
                self.sDV.wait(v, self.sDV.v)
                self.sAC.wait(v, self.sAC.v)
                self.sPE.wait(v, self.sPE.v)
                self.sIn.wait(v, self.sIn.v)
                self.sSc.wait(v, self.sSc.v)
                self.sW.wait(v, self.sW.v)
                v.memset(z[:, :], 0.0)
                self.C["fin_z"] = self.sDV.inc(v.last)
            self.v(v_fin)

            def sp_fin(sp):
                self.sDV.wait(sp, self.C["fin_z"])
                for k in range(8):
                    for tc in range(4):
                        self.sOut.dma(sp.dma_start(
                            out=self.outT[k * 128:(k + 1) * 128, tc * 512:(tc + 1) * 512], in_=z[:, :]))
                self.sOut.wait(sp, self.sOut.v)
            self.sp(sp_fin)

    def phase_p0(self, AE):
        nc, C = self.nc, self.C
        sW, sIn, sSc, sPE, sDV, sAC = self.sW, self.sIn, self.sSc, self.sPE, self.sDV, self.sAC
        self.w_in_sb = AE(nc.sbuf_tensor("w_in_sb", [128, 8 * DPROJ], BF16))
        xn = self.xn
        self.p0_stack = ExitStack()
        P0 = self.p0_stack.enter_context
        xbuf = P0(nc.sbuf_tensor("xbuf", [128, 2 * S], F32))
        sqb = P0(nc.sbuf_tensor("sqb", [128, 2 * S], F32))
        mu_r = P0(nc.sbuf_tensor("mu_r", [1, S], F32))
        var_r = P0(nc.sbuf_tensor("var_r", [1, S], F32))
        tmp_r = P0(nc.sbuf_tensor("tmp_r", [1, S], F32))
        bcb = P0(nc.sbuf_tensor("bcb", [128, 2 * S], F32))
        ps_sum = P0(nc.psum_tensor("ps_sum", [1, S], F32))
        ps_sq = P0(nc.psum_tensor("ps_sq", [1, S], F32))

        def g0(g):
            for dst, src in [(self.idb, self.cid), (self.idf, self.cid), (self.trif, self.ctri),
                             (self.convw_sb, self.convw), (self.convb_sb, self.convb)]:
                sW.dma(g.dma_start(out=dst[:, :], in_=src[:, :]))
            sW.dma(g.dma_start(out=self.vecs[:, 0:16], in_=self.drepd[:, :]))
            sW.dma(g.dma_start(out=self.vecs[:, 16:32], in_=self.normwd[:, :]))
            sW.dma(g.dma_start(out=self.vecs[:, 32:40], in_=self.lngd[:, :]))
            sW.dma(g.dma_start(out=self.vecs[:, 40:48], in_=self.lnbd[:, :]))
            sW.dma(g.dma_start(out=self.smv[:, 0:1], in_=self.dtbias[:, :]))
            sW.dma(g.dma_start(out=self.smv[:, 2:3], in_=self.alog[:, :]))
            sW.dma(g.dma_start(out=self.b1_sb[:, 0:32], in_=self.b1t[0, :, :]))
            sW.dma(g.dma_start(out=self.b1_sb[:, 32:64], in_=self.b1t[1, :, :]))
            sW.dma(g.dma_start(out=self.b2_sb[:, 0:8], in_=self.b2t[0, :, :]))
            sW.dma(g.dma_start(out=self.b2_sb[:, 8:16], in_=self.b2t[1, :, :]))
            C["consts_w"] = sW.v
            for k in range(8):
                sW.dma(g.dma_start(out=self.w_in_sb[:, k * DPROJ:(k + 1) * DPROJ],
                                   in_=self.w_inT[k * 128:(k + 1) * 128, :]))
            C["w_in_w"] = sW.v
        self.g(g0)

        def v00(v):
            v.memset(self.ones_c[:, :], 1.0)
            v.memset(self.ones_cb[:, :], 1.0)
            v.memset(self.epsc[:, :], EPS)
            v.memset(self.cb4T[:, :], 0.0)
            for j in range(4):
                v.memset(self.cb4T[32 * j:32 * (j + 1), j:j + 1], 1.0)
            sW.wait(v, C["consts_w"])
            v.tensor_scalar_mul(self.convbh[:, :], self.convb_sb[:, :], 0.5)
            C["consts_v"] = sDV.inc(v.last)
        self.v(v00)

        for k in range(8):
            def s_x1(sp, k=k):
                if k >= 2:
                    sPE.wait(sp, C["p0_mm_%d" % (k - 2)])
                C["p0_x1_%d" % k] = sIn.dma(
                    sp.dma_start(out=xbuf[:, (k % 2) * S:(k % 2) * S + S],
                                 in_=self.xT[k * 128:(k + 1) * 128, :]))
            self.sp(s_x1)

            def a_sq(s, k=k):
                sIn.wait(s, C["p0_x1_%d" % k])
                if k >= 2:
                    sPE.wait(s, C["p0_mm_%d" % (k - 2)])
                s.activation(sqb[:, (k % 2) * S:(k % 2) * S + S], xbuf[:, (k % 2) * S:(k % 2) * S + S],
                             AF.Square)
                C["p0_sq_%d" % k] = sAC.inc(s.last)
            self.s(a_sq)

            def t_mm(t, k=k):
                if k == 0:
                    sDV.wait(t, C["consts_v"])
                sIn.wait(t, C["p0_x1_%d" % k])
                sAC.wait(t, C["p0_sq_%d" % k])
                for tc in range(4):
                    t.matmul(ps_sum[0:1, tc * 512:(tc + 1) * 512], self.ones_c[:, :],
                             xbuf[:, (k % 2) * S + tc * 512:(k % 2) * S + tc * 512 + 512],
                             start=(k == 0), stop=(k == 7))
                    t.matmul(ps_sq[0:1, tc * 512:(tc + 1) * 512], self.ones_c[:, :],
                             sqb[:, (k % 2) * S + tc * 512:(k % 2) * S + tc * 512 + 512],
                             start=(k == 0), stop=(k == 7))
                C["p0_mm_%d" % k] = sPE.inc(t.last)
            self.t(t_mm)

        def v_ev(v):
            sPE.wait(v, C["p0_mm_7"])
            v.tensor_scalar_mul(mu_r[0:1, :], ps_sum[0:1, :], 1.0 / D)
            v.tensor_scalar_mul(var_r[0:1, :], ps_sq[0:1, :], 1.0 / D)
            v.tensor_mul(tmp_r[0:1, :], mu_r[0:1, :], mu_r[0:1, :])
            v.tensor_sub(var_r[0:1, :], var_r[0:1, :], tmp_r[0:1, :])
            C["p0_ev"] = sDV.inc(v.last)
        self.v(v_ev)

        def a_rs(s):
            sDV.wait(s, C["p0_ev"])
            s.activation(var_r[0:1, :], var_r[0:1, :], AF.Ln, bias=self.epsc[0:1, 0:1])
            s.activation(var_r[0:1, :], var_r[0:1, :], AF.Exp, scale=-0.5)
            C["p0_rs"] = sAC.inc(s.last)
        self.s(a_rs)

        def s_rw(sp):
            sAC.wait(sp, C["p0_rs"])
            sIn.dma(sp.dma_start(out=self.s_rows[0:1, :], in_=mu_r[0:1, :]))
            C["p0_rw"] = sIn.dma(sp.dma_start(out=self.s_rows[1:2, :], in_=var_r[0:1, :]))
        self.sp(s_rw)

        for k in range(8):
            def s_bc(sp, k=k):
                if k == 0:
                    sIn.wait(sp, C["p0_rw"])
                    sIn.dma(sp.dma_start(out=bcb[:, 0:S],
                                         in_=self.s_rows[0:1, :].broadcast_to([128, S])))
                    C["p0_bc"] = sIn.dma(
                        sp.dma_start(out=bcb[:, S:2 * S],
                                     in_=self.s_rows[1:2, :].broadcast_to([128, S])))
                if k < 2:
                    sPE.wait(sp, C["p0_mm_7"])
                    sAC.wait(sp, C["p0_sq_7"])
                else:
                    sDV.wait(sp, C["p0_xn_%d" % (k - 2)])
                C["p0_x2_%d" % k] = sIn.dma(
                    sp.dma_start(out=xbuf[:, (k % 2) * S:(k % 2) * S + S],
                                 in_=self.xT[k * 128:(k + 1) * 128, :]))
            self.sp(s_bc)

            def v_xn(v, k=k):
                sIn.wait(v, C["p0_x2_%d" % k])
                sIn.wait(v, C["p0_bc"])
                if k >= 1:
                    sAC.wait(v, C["p0_sq2_%d" % (k - 1)])
                sl = slice(k * S, (k + 1) * S)
                xb = xbuf[:, (k % 2) * S:(k % 2) * S + S]
                v.tensor_sub(sqb[:, 0:S], xb, bcb[:, 0:S])
                v.tensor_mul(xn[:, sl], sqb[:, 0:S], bcb[:, S:2 * S])
                C["p0_xn_%d" % k] = sDV.inc(v.last)
            self.v(v_xn)

            def a_sq2(s, k=k):
                sDV.wait(s, C["p0_xn_%d" % k])
                if k >= 1:
                    sPE.wait(s, C["p0_mm2_%d" % (k - 1)])
                s.activation(sqb[:, S:2 * S], xn[:, k * S:(k + 1) * S], AF.Square)
                C["p0_sq2_%d" % k] = sAC.inc(s.last)
            self.s(a_sq2)

            def t_mm2(t, k=k):
                sAC.wait(t, C["p0_sq2_%d" % k])
                for tc in range(4):
                    t.matmul(ps_sum[0:1, tc * 512:(tc + 1) * 512], self.ones_c[:, :],
                             sqb[:, S + tc * 512: S + tc * 512 + 512],
                             start=(k == 0), stop=(k == 7))
                C["p0_mm2_%d" % k] = sPE.inc(t.last)
            self.t(t_mm2)

        def v_ev2(v):
            sPE.wait(v, C["p0_mm2_7"])
            v.tensor_scalar_mul(tmp_r[0:1, :], ps_sum[0:1, :], 1.0 / D)
            C["p0_ev2"] = sDV.inc(v.last)
        self.v(v_ev2)

        def a_rs2(s):
            sDV.wait(s, C["p0_ev2"])
            s.activation(tmp_r[0:1, :], tmp_r[0:1, :], AF.Ln, bias=self.epsc[0:1, 0:1])
            s.activation(tmp_r[0:1, :], tmp_r[0:1, :], AF.Exp, scale=-0.5)
            C["p0_rs2"] = sAC.inc(s.last)
        self.s(a_rs2)

        def s_rw2(sp):
            sAC.wait(sp, C["p0_rs2"])
            C["p0_rw2"] = sIn.dma(sp.dma_start(out=self.s_rows[2:3, :], in_=tmp_r[0:1, :]))
        self.sp(s_rw2)

        C["p0_dv"] = sDV.v
        C["p0_ac"] = sAC.v
        C["p0_pe"] = sPE.v

        if self.debug >= 1:
            def gdbg(g):
                sDV.wait(g, C["p0_dv"])
                sAC.wait(g, C["p0_rs2"])
                for k in range(8):
                    self.sW.dma(g.dma_start(out=self.dbg["xn"][k, :, :], in_=xn[:, k * S:(k + 1) * S]))
                self.sW.dma(g.dma_start(out=self.dbg["rstd"][0:1, :], in_=var_r[0:1, :]))
                self.sW.dma(g.dma_start(out=self.dbg["rstd"][1:2, :], in_=tmp_r[0:1, :]))
            self.g(gdbg)

    def phase_p1(self, AE):
        nc, C = self.nc, self.C
        sW, sIn, sSc, sPE, sDV, sAC = self.sW, self.sIn, self.sSc, self.sPE, self.sDV, self.sAC
        xn = self.xn
        self.p0_stack.close()
        with ExitStack() as p1s:
            PP = p1s.enter_context
            csR = PP(nc.sbuf_tensor("csR", [32, S], F32))
            ddR = PP(nc.sbuf_tensor("ddR", [32, S], F32))
            dtpR = PP(nc.sbuf_tensor("dtpR", [32, S], F32))
            esb = PP(nc.sbuf_tensor("esb", [32, S], F32))
            ps_dt = PP(nc.psum_tensor("ps_dt", [32, S], F32))
            ps_t1 = PP(nc.psum_tensor("ps_t1", [128, 512], F32))
            ps_t2 = PP(nc.psum_tensor("ps_t2", [128, 512], F32))

            def t_p1a(t):
                sW.wait(t, C["w_in_w"])
                sDV.wait(t, C["p0_dv"])
                for kt in range(8):
                    for tc in range(4):
                        t.matmul(ps_dt[:, tc * 512:(tc + 1) * 512],
                                 self.w_in_sb[:, kt * DPROJ + 4352: kt * DPROJ + 4384],
                                 xn[:, kt * S + tc * 512: kt * S + tc * 512 + 512],
                                 start=(kt == 0), stop=(kt == 7))
                C["p1_dt"] = sPE.inc(t.last)
            self.t(t_p1a)

            def a_p1(s):
                sPE.wait(s, C["p1_dt"])
                s.activation(esb[:, :], ps_dt[:, :], AF.Exp, bias=self.smv[:, 0:1])
                s.activation(dtpR[:, :], esb[:, :], AF.Ln, bias=self.ones_c[0:32, 0:1])
                s.activation(self.smv[:, 1:2], self.smv[:, 2:3], AF.Exp)
                C["p1_dtp"] = sAC.inc(s.last)
            self.s(a_p1)

            def v_p1(v):
                sAC.wait(v, C["p1_dtp"])
                v.tensor_scalar(esb[:, :], dtpR[:, :], self.smv[:, 1:2], -1.0, ALU.mult, ALU.mult)
                for c in range(NCH):
                    v.tensor_tensor_scan(csR[:, c * CH:(c + 1) * CH],
                                         esb[:, c * CH:(c + 1) * CH], esb[:, c * CH:(c + 1) * CH],
                                         0.0, ALU.add, ALU.bypass)
                for c in range(NCH):
                    v.tensor_scalar(ddR[:, c * CH:(c + 1) * CH],
                                    csR[:, c * CH:(c + 1) * CH],
                                    csR[:, c * CH + CH - 1: c * CH + CH], None, ALU.subtract)
                C["p1_cs"] = sDV.inc(v.last)
            self.v(v_p1)

            def a_p1b(s):
                sDV.wait(s, C["p1_cs"])
                s.activation(ddR[:, :], ddR[:, :], AF.Exp, scale=-1.0)
                C["p1_dec"] = sAC.inc(s.last)
            self.s(a_p1b)

            def v_p1b(v):
                sAC.wait(v, C["p1_dec"])
                v.tensor_mul(ddR[:, :], ddR[:, :], dtpR[:, :])
                C["p1_dd"] = sDV.inc(v.last)
            self.v(v_p1b)

            def s_cs(sp):
                sDV.wait(sp, C["p1_dd"])
                C["p1_csw"] = sIn.dma(sp.dma_start(out=self.s_cs[:, :], in_=csR[:, :]))
            self.sp(s_cs)

            srcRs = [csR, ddR, dtpR]
            for blki in range(16):
                def t_tr(t, blki=blki):
                    if blki == 0:
                        sDV.wait(t, C["p1_dd"])
                    c, st = blki // 2, blki % 2
                    ps = ps_t1 if blki % 2 == 0 else ps_t2
                    if blki >= 2:
                        sDV.wait(t, C["p1_ct_%d" % (blki - 2)])
                    for ri, rt in enumerate(srcRs):
                        t.transpose(ps[:, ri * 32:(ri + 1) * 32],
                                    rt[:, c * CH + st * 128: c * CH + st * 128 + 128],
                                    self.idf[0:32, 0:32])
                    C["p1_tr_%d" % blki] = sPE.inc(t.last)
                self.t(t_tr)

                def v_tr(v, blki=blki):
                    ps = ps_t1 if blki % 2 == 0 else ps_t2
                    sPE.wait(v, C["p1_tr_%d" % blki])
                    v.tensor_copy(self.colT[:, blki * 96:(blki + 1) * 96], ps[:, 0:96])
                    C["p1_ct_%d" % blki] = sDV.inc(v.last)
                self.v(v_tr)

            if self.debug >= 1:
                def gdbg1(g):
                    sDV.wait(g, C["p1_dd"])
                    self.sW.dma(g.dma_start(out=self.dbg["dtp"][:, :], in_=dtpR[:, :]))
                    self.sW.dma(g.dma_start(out=self.dbg["cs"][:, :], in_=csR[:, :]))
                    self.sW.dma(g.dma_start(out=self.dbg["dd"][:, :], in_=ddR[:, :]))
                    self.sDV.wait(g, C["p1_ct_15"])
                self.g(gdbg1)
        C["p1_dv"] = sDV.v
        C["p1_ac"] = sAC.v
        C["p1_pe"] = sPE.v

    def phase_m1_m2(self, AE):
        nc, C = self.nc, self.C
        sDV, sSc, sAC, sIn = self.sDV, self.sSc, self.sAC, self.sIn

        self.xbc = AE(nc.sbuf_tensor("xbc", [128, 18 * XBCW], BF16))
        self.zstg = AE(nc.sbuf_tensor("zstg", [128, 4 * 512], BF16))
        self.zc = AE(nc.sbuf_tensor("zc", [128, 16 * CH], BF16))
        self.thz = AE(nc.sbuf_tensor("thz", [128, 8 * CH], BF16))
        self.uall = AE(nc.sbuf_tensor("uall", [128, 16 * CH], BF16))
        self.xpost = AE(nc.sbuf_tensor("xpost", [128, 16 * CH], BF16))
        self.bct = AE(nc.sbuf_tensor("bct", [128, 2 * CH], BF16))
        self.x_tok = [AE(nc.sbuf_tensor("x_tok%d" % i, [128, DIN], BF16)) for i in range(2)]
        self.Bmt = AE(nc.sbuf_tensor("Bmt", [128, 2 * 128], BF16))
        self.carry = AE(nc.sbuf_tensor("carry", [128, DIN], F32))
        self.carry_bf = AE(nc.sbuf_tensor("carry_bf", [128, DIN], BF16))
        self.Gm = AE(nc.sbuf_tensor("Gm", [128, 384], BF16))
        self.pre_c = AE(nc.sbuf_tensor("pre_c", [128, 2 * CH], F32))
        self.thc = AE(nc.sbuf_tensor("thc", [128, 2 * CH], BF16))
        self.Lsb = AE(nc.sbuf_tensor("Lsb", [128, 2 * 384], BF16))
        self.Mt_sb = AE(nc.sbuf_tensor("Mt_sb", [128, 2 * 384], BF16))
        self.Mv_sb = AE(nc.sbuf_tensor("Mv_sb", [128, 2 * CH], BF16))
        self.Ecsb = AE(nc.sbuf_tensor("Ecsb", [128, 2 * CH], BF16))
        self.xddb = AE(nc.sbuf_tensor("xddb", [128, 2 * 128], BF16))
        self.tbc = AE(nc.sbuf_tensor("tbc", [128, 16 * CH], BF16))
        self.sqt = AE(nc.sbuf_tensor("sqt", [128, 2 * CH], BF16))
        self.ytmp = AE(nc.sbuf_tensor("ytmp", [128, CH], F32))
        self.csb = AE(nc.sbuf_tensor("csb", [128, 2 * CH], F32))
        self.cdb = AE(nc.sbuf_tensor("cdb", [128, 256], F32))
        self.msrow = AE(nc.sbuf_tensor("msrow", [1, CH], F32))

        self.psA = AE(nc.psum_tensor("psA", [128, 512], F32))
        self.psB = AE(nc.psum_tensor("psB", [128, 512], F32))
        self.psC = AE(nc.psum_tensor("psC", [128, 512], F32))
        self.psD = AE(nc.psum_tensor("psD", [128, 512], F32))
        self.psE = AE(nc.psum_tensor("psE", [128, 1024], BF16))
        self.psF = AE(nc.psum_tensor("psF", [128, 1024], BF16))

        def s_cdb(sp):
            sIn.wait(sp, C["p1_csw"])
            C["cdb_l"] = sSc.dma(sp.dma_start(
                out=self.cdb[:, :].rearrange("p (h c) -> p h c", h=32),
                in_=self.s_cs[:, CH - 1::CH].rearrange("h c -> () h c").broadcast_to([128, 32, 8]),
                allow_slow_non_contiguous=True))
        self.sp(s_cdb)

        def a_cdb(s):
            sSc.wait(s, C["cdb_l"])
            s.activation(self.cdb[:, :], self.cdb[:, :], AF.Exp)
            C["cdb"] = sAC.inc(s.last)
        self.s(a_cdb)

        def v_init(v):
            if self.debug:
                self.sW.wait(v, self.sW.v)
            v.memset(self.carry[:, :], 0.0)
            v.memset(self.carry_bf[:, :], 0.0)
            for t18 in range(18):
                v.memset(self.xbc[:, t18 * XBCW: t18 * XBCW + 3], 0.0)
            C["m2_init"] = sDV.inc(v.last)
        self.v(v_init)
        for h in range(32):
            C["carry_-1_%d" % h] = C["m2_init"]

        for tc in range(4):
            self._emit_m1_tc(tc)
            for c in (2 * tc, 2 * tc + 1):
                self._emit_m2_chunk(c)
            if tc < 3:
                def v_hist(v, tc=tc):
                    for t18 in range(18):
                        v.tensor_copy(self.xbc[:, t18 * XBCW: t18 * XBCW + 3],
                                      self.xbc[:, t18 * XBCW + 512: t18 * XBCW + 515])
                    C["m2_hist_%d" % tc] = sDV.inc(v.last)
                self.v(v_hist)

        C["m2_dv"] = sDV.v
        C["m2_pe"] = self.sPE.v
        C["m2_ac"] = self.sAC.v
        C["m2_td"] = C["m2_tdma_%d" % (NCH - 1)]
        C["m2_msw"] = C["msw_%d" % (NCH - 1)]

    def _emit_m1_tc(self, tc):
        nc, C = self.nc, self.C
        sW, sIn, sPE, sDV, sAC = self.sW, self.sIn, self.sPE, self.sDV, self.sAC
        xn, w_in_sb = self.xn, self.w_in_sb
        xbc, zstg = self.xbc, self.zstg
        ps_zx = [self.psA, self.psB]

        for mt in range(34):
            def t_m1(t, mt=mt, tc=tc):
                i = tc * 34 + mt
                if i == 0:
                    sAC.wait(t, C["p1_ac"])
                    sDV.wait(t, C["p1_dv"])
                if mt == 0 and tc > 0:
                    sDV.wait(t, C["carry_%d_%d" % (2 * tc - 1, 31)])
                    sDV.wait(t, C["tpair_ev_%d_%d" % (2 * tc - 1, 14)])
                    sDV.wait(t, C["tpair_ev_%d_%d" % (2 * tc - 1, 15)])
                if i >= 2:
                    sDV.wait(t, C["m1_ev_%d" % (i - 2)])
                ps = ps_zx[i % 2]
                for kt in range(8):
                    t.matmul(ps[:, :],
                             w_in_sb[:, kt * DPROJ + mt * 128: kt * DPROJ + (mt + 1) * 128],
                             xn[:, kt * S + tc * 512: kt * S + tc * 512 + 512],
                             start=(kt == 0), stop=(kt == 7))
                C["m1_mm_%d" % i] = sPE.inc(t.last)
            self.t(t_m1)

            def v_m1(v, mt=mt, tc=tc):
                i = tc * 34 + mt
                if i == 0:
                    sDV.wait(v, C["m2_init"])
                sPE.wait(v, C["m1_mm_%d" % i])
                if mt < 16:
                    zi = tc * 16 + mt
                    if zi >= 4:
                        sIn.wait(v, C["m1_zd_%d" % (zi - 4)])
                    v.tensor_copy(zstg[:, (zi % 4) * 512:(zi % 4) * 512 + 512], ps_zx[i % 2][:, :])
                else:
                    v.tensor_copy(xbc[:, (mt - 16) * XBCW + 3: (mt - 16) * XBCW + 3 + 512],
                                  ps_zx[i % 2][:, :])
                C["m1_ev_%d" % i] = sDV.inc(v.last)
            self.v(v_m1)

            if mt < 16:
                def s_m1(sp, mt=mt, tc=tc):
                    i = tc * 34 + mt
                    zi = tc * 16 + mt
                    sDV.wait(sp, C["m1_ev_%d" % i])
                    C["m1_zd_%d" % zi] = sIn.dma(
                        sp.dma_start(out=self.s_z[mt, :, tc * 512:(tc + 1) * 512],
                                     in_=zstg[:, (zi % 4) * 512:(zi % 4) * 512 + 512]))
                self.sp(s_m1)
        C["m1_done_%d" % tc] = C["m1_ev_%d" % (tc * 34 + 33)]
        C["m1_zall_%d" % tc] = C["m1_zd_%d" % (tc * 16 + 15)]

    def _emit_m2_chunk(self, c):
        nc, C = self.nc, self.C
        sW, sIn, sSc, sPE, sDV, sAC = self.sW, self.sIn, self.sSc, self.sPE, self.sDV, self.sAC
        colT, cdb = self.colT, self.cdb
        xbc, zc, thz, uall = self.xbc, self.zc, self.thz, self.uall
        xpost, bct, x_tok, Bmt = self.xpost, self.bct, self.x_tok, self.Bmt
        carry, carry_bf, Gm = self.carry, self.carry_bf, self.Gm
        pre_c, thc = self.pre_c, self.thc
        Lsb, Mt_sb, Mv_sb, Ecsb = self.Lsb, self.Mt_sb, self.Mv_sb, self.Ecsb
        xddb, tbc, sqt, ytmp, csb = self.xddb, self.tbc, self.sqt, self.ytmp, self.csb
        psA, psB, psC, psD = self.psA, self.psB, self.psC, self.psD
        ps_tr = [self.psE, self.psF]
        ps_y = [self.psB, self.psC]
        lc = c % 2

        def s_zl(sp, c=c):
            sIn.wait(sp, C["m1_zall_%d" % (c // 2)])
            if c >= 1:
                sDV.wait(sp, C["m2_u_%d_1" % (c - 1)])
                sAC.wait(sp, C["m2_thz_%d_1" % (c - 1)])
            C["m2_zl_%d" % c] = sIn.dma(
                sp.dma_start(out=zc[:, :].rearrange("p (a b) -> p a b", a=16),
                             in_=self.s_z[:, :, c * CH:(c + 1) * CH]
                             .rearrange("a p b -> p a b")))
        self.sp(s_zl)

        base_thz = sAC.v
        C["m2_thz_%d_0" % c] = base_thz + 1
        C["m2_thz_%d_1" % c] = base_thz + 2
        sAC.v = base_thz + 2

        for hf in range(2):
            def v_zs(v, c=c, hf=hf):
                sAC.wait(v, C["m2_thz_%d_%d" % (c, hf)])
                v.scalar_tensor_tensor(uall[:, hf * 8 * CH:(hf + 1) * 8 * CH], thz[:, :], 1.0,
                                       zc[:, hf * 8 * CH:(hf + 1) * 8 * CH], ALU.add, ALU.mult)
                C["m2_u_%d_%d" % (c, hf)] = sDV.inc(v.last)
            self.v(v_zs)

        def a_zs_all(s, c=c):
            for hf in range(2):
                sIn.wait(s, C["m2_zl_%d" % c])
                if hf == 1:
                    sDV.wait(s, C["m2_u_%d_0" % c])
                elif c >= 1:
                    sDV.wait(s, C["m2_u_%d_1" % (c - 1)])
                s.activation(thz[:, :], zc[:, hf * 8 * CH:(hf + 1) * 8 * CH], AF.Tanh, scale=0.5)
                s.last.then_inc(sAC.h, 1)
        self.s(a_zs_all)

        base_cvt = sAC.v
        for t18 in range(18):
            C["cv_th_%d_%d" % (c, t18)] = base_cvt + t18 + 1
        sAC.v = base_cvt + 18
        for t18 in range(18):
            def v_cv(v, t18=t18, c=c):
                p = t18 % 2
                if t18 == 0:
                    sDV.wait(v, C["m1_done_%d" % (c // 2)])
                    if c >= 1:
                        sPE.wait(v, C["hy_%d_%d" % (c - 1, 31)])
                base = t18 * XBCW + lc * CH
                sl = lambda k: xbc[:, base + k: base + k + CH]
                v.tensor_scalar_mul(pre_c[:, p * CH:(p + 1) * CH], sl(0),
                                    self.convw_sb[:, t18 * 4: t18 * 4 + 1])
                for k in (1, 2, 3):
                    v.scalar_tensor_tensor(pre_c[:, p * CH:(p + 1) * CH], sl(k),
                                           self.convw_sb[:, t18 * 4 + k: t18 * 4 + k + 1],
                                           pre_c[:, p * CH:(p + 1) * CH], ALU.mult, ALU.add)
                C["cv_pre_%d_%d" % (c, t18)] = sDV.inc(v.last)
            self.v(v_cv)

            def v_cv2(v, t18=t18, c=c):
                p = t18 % 2
                sAC.wait(v, C["cv_th_%d_%d" % (c, t18)])
                v.tensor_scalar(pre_c[:, p * CH:(p + 1) * CH], pre_c[:, p * CH:(p + 1) * CH],
                                self.convb_sb[:, t18:t18 + 1], 0.5, ALU.add, ALU.mult)
                if t18 < 16:
                    dest = xpost[:, t18 * CH:(t18 + 1) * CH]
                else:
                    dest = bct[:, (t18 - 16) * CH:(t18 - 15) * CH]
                v.scalar_tensor_tensor(dest, thc[:, p * CH:(p + 1) * CH], 1.0,
                                       pre_c[:, p * CH:(p + 1) * CH], ALU.add, ALU.mult)
                C["cv_out_%d_%d" % (c, t18)] = sDV.inc(v.last)
            self.v(v_cv2)

        def a_cv_all(s, c=c):
            for t18 in range(18):
                p = t18 % 2
                sDV.wait(s, C["cv_pre_%d_%d" % (c, t18)])
                if t18 >= 2:
                    sDV.wait(s, C["cv_out_%d_%d" % (c, t18 - 2)])
                s.activation(thc[:, p * CH:(p + 1) * CH], pre_c[:, p * CH:(p + 1) * CH],
                             AF.Tanh, scale=0.5, bias=self.convbh[:, t18:t18 + 1])
                s.last.then_inc(sAC.h, 1)
        self.s(a_cv_all)
        C["cv_done_%d" % c] = C["cv_out_%d_%d" % (c, 17)]

        for grp in range(4):
            def t_tp(t, grp=grp, c=c):
                st, half = grp // 2, grp % 2
                ps = ps_tr[grp % 2]
                sDV.wait(t, C["cv_out_%d_%d" % (c, (half + 1) * 8 - 1)])
                if grp >= 2:
                    sDV.wait(t, C["tp_ev_%d_%d" % (c, grp - 2)])
                elif c >= 1:
                    if grp == 0:
                        sDV.wait(t, C["bm_ev_%d" % (c - 1)])
                    else:
                        sDV.wait(t, C["tp_ev_%d_%d" % (c - 1, 3)])
                for j in range(8):
                    dt = half * 8 + j
                    t.transpose(ps[:, j * 128:(j + 1) * 128],
                                xpost[:, dt * CH + st * 128: dt * CH + st * 128 + 128], self.idb[:, :])
                C["tp_mm_%d_%d" % (c, grp)] = sPE.inc(t.last)
            self.t(t_tp)

            def v_tp(v, grp=grp, c=c):
                st, half = grp // 2, grp % 2
                ps = ps_tr[grp % 2]
                sPE.wait(v, C["tp_mm_%d_%d" % (c, grp)])
                v.tensor_copy(x_tok[st][:, half * 1024:(half + 1) * 1024], ps[:, :])
                C["tp_ev_%d_%d" % (c, grp)] = sDV.inc(v.last)
            self.v(v_tp)

        def t_bm(t, c=c):
            ps = ps_tr[0]
            sDV.wait(t, C["tp_ev_%d_%d" % (c, 2)])
            sDV.wait(t, C["cv_done_%d" % c])
            for st in range(2):
                t.transpose(ps[:, st * 128:(st + 1) * 128], bct[:, st * 128:(st + 1) * 128], self.idb[:, :])
            C["bm_mm_%d" % c] = sPE.inc(t.last)
        self.t(t_bm)

        def v_bm(v, c=c):
            sPE.wait(v, C["bm_mm_%d" % c])
            v.tensor_copy(Bmt[:, :], ps_tr[0][:, 0:256])
            C["bm_ev_%d" % c] = sDV.inc(v.last)
        self.v(v_bm)

        def t_g(t, c=c):
            sDV.wait(t, C["cv_done_%d" % c])
            if c >= 1:
                sDV.wait(t, C["ms_ev_%d" % (c - 1)])
            t.matmul(psD[:, 0:256], bct[:, 0:128], bct[:, 256:512], start=True, stop=False)
            t.matmul(psD[:, 256:384], bct[:, 128:256], bct[:, 384:512], start=False, stop=True)
            C["g_mm_%d" % c] = sPE.inc(t.last)
        self.t(t_g)

        def v_g(v, c=c):
            sPE.wait(v, C["g_mm_%d" % c])
            v.tensor_mul(Gm[:, 0:128], psD[:, 0:128], self.trif[:, :])
            v.tensor_copy(Gm[:, 128:256], psD[:, 128:256])
            v.tensor_mul(Gm[:, 256:384], psD[:, 256:384], self.trif[:, :])
            C["g_ev_%d" % c] = sDV.inc(v.last)
        self.v(v_g)

        base_hx = sAC.v
        for h in range(32):
            C["hx_%d_%d" % (c, h)] = base_hx + h + 1
        sAC.v = base_hx + 32
        base_sq = sAC.v
        for pp16 in range(16):
            C["sqt_%d_%d" % (c, pp16)] = base_sq + pp16 + 1
        sAC.v = base_sq + 16

        for h in range(32):
            par = h % 2
            pp = h // 2

            def s_csb(s, h=h, c=c, par=par):
                if h == 0 and c == 0:
                    sIn.wait(s, C["p1_csw"])
                if h >= 2:
                    sAC.wait(s, C["hx_%d_%d" % (c, h - 2)])
                    sDV.wait(s, C["hl_%d_%d" % (c, h - 2)])
                elif c >= 1:
                    sAC.wait(s, C["hx_%d_%d" % (c - 1, 30 + h)])
                    sDV.wait(s, C["hl_%d_%d" % (c - 1, 30 + h)])
                C["hcb_%d_%d" % (c, h)] = sSc.dma(
                    s.dma_start(out=csb[:, par * CH:(par + 1) * CH],
                                in_=self.s_cs[h:h + 1, c * CH:(c + 1) * CH].broadcast_to([128, CH])))
            self.sp(s_csb)

            def v_lp(v, h=h, c=c, par=par):
                sSc.wait(v, C["hcb_%d_%d" % (c, h)])
                blk0, blk1 = (2 * c) * 96, (2 * c + 1) * 96
                v.tensor_scalar(Lsb[:, par * 384: par * 384 + 256], csb[:, par * CH:(par + 1) * CH],
                                colT[:, blk0 + h: blk0 + h + 1], 0.0, ALU.subtract, ALU.min)
                v.tensor_scalar(Lsb[:, par * 384 + 256: par * 384 + 384],
                                csb[:, par * CH + 128:(par + 1) * CH],
                                colT[:, blk1 + h: blk1 + h + 1], 0.0, ALU.subtract, ALU.min)
                C["hl_%d_%d" % (c, h)] = sDV.inc(v.last)
            self.v(v_lp)

            def v_m(v, h=h, c=c, par=par):
                sAC.wait(v, C["hx_%d_%d" % (c, h)])
                if h == 0:
                    sDV.wait(v, C["g_ev_%d" % c])
                    sDV.wait(v, C["tp_ev_%d_3" % c])
                    sDV.wait(v, C["bm_ev_%d" % c])
                blk0, blk1 = (2 * c) * 96, (2 * c + 1) * 96
                v.scalar_tensor_tensor(Mt_sb[:, par * 384: par * 384 + 256],
                                       Lsb[:, par * 384: par * 384 + 256],
                                       colT[:, blk0 + 64 + h: blk0 + 64 + h + 1],
                                       Gm[:, 0:256], ALU.mult, ALU.mult)
                v.scalar_tensor_tensor(Mt_sb[:, par * 384 + 256: par * 384 + 384],
                                       Lsb[:, par * 384 + 256: par * 384 + 384],
                                       colT[:, blk1 + 64 + h: blk1 + 64 + h + 1],
                                       Gm[:, 256:384], ALU.mult, ALU.mult)
                v.tensor_mul(Mv_sb[:, par * CH:(par + 1) * CH], Ecsb[:, par * CH:(par + 1) * CH],
                             bct[:, 256:512])
                v.tensor_scalar_mul(xddb[:, par * 128: par * 128 + 64],
                                    x_tok[0][:, h * 64:(h + 1) * 64],
                                    colT[:, blk0 + 32 + h: blk0 + 32 + h + 1])
                v.tensor_scalar_mul(xddb[:, par * 128 + 64: par * 128 + 128],
                                    x_tok[1][:, h * 64:(h + 1) * 64],
                                    colT[:, blk1 + 32 + h: blk1 + 32 + h + 1])
                C["hm_%d_%d" % (c, h)] = sDV.inc(v.last)
            self.v(v_m)

            if self.debug >= 1 and c == 0 and h in (0, 1):
                def gdbgh(g, h=h, c=c, par=par):
                    sDV.wait(g, C["hm_%d_%d" % (c, h)])
                    self.sW.dma(g.dma_start(out=self.dbg["mt"][par, :, :], in_=Mt_sb[:, par * 384:(par + 1) * 384]))
                    self.sW.dma(g.dma_start(out=self.dbg["mv"][par, :, :], in_=Mv_sb[:, par * CH:(par + 1) * CH]))
                    self.sW.dma(g.dma_start(out=self.dbg["lsb"][par, :, :], in_=Lsb[:, par * 384:(par + 1) * 384]))
                    self.sW.dma(g.dma_start(out=self.dbg["csb"][par, :, :], in_=csb[:, par * CH:(par + 1) * CH]))
                    self.sW.dma(g.dma_start(out=self.dbg["ecs"][par, :, :], in_=Ecsb[:, par * CH:(par + 1) * CH]))
                    if h == 0:
                        self.sW.dma(g.dma_start(out=self.dbg["gm"][:, :], in_=Gm[:, :]))
                        self.sW.dma(g.dma_start(out=self.dbg["bmt"][:, :], in_=Bmt[:, :]))
                        self.sW.dma(g.dma_start(out=self.dbg["colt"][:, :], in_=colT[:, :]))
                        self.sW.dma(g.dma_start(out=self.dbg["xtok"][0, :, :], in_=x_tok[0][:, :]))
                        self.sW.dma(g.dma_start(out=self.dbg["xtok"][1, :, :], in_=x_tok[1][:, :]))
                    C["dbgh_%d" % h] = self.sW.v
                self.g(gdbgh)
                def v_hold(v, h=h):
                    self.sW.wait(v, C["dbgh_%d" % h])
                    v.nop()
                    C["hl_hold_%d" % h] = sDV.inc(v.last)
                self.v(v_hold)

            if par == 1:
                def t_y(t, h=h, c=c, pp=pp):
                    hA, hB = h - 1, h
                    sDV.wait(t, C["hm_%d_%d" % (c, hB)])
                    sDV.wait(t, C["carry_%d_%d" % (c - 1, hA)])
                    sDV.wait(t, C["carry_%d_%d" % (c - 1, hB)])
                    py = ps_y[pp % 2]
                    if pp >= 2:
                        sDV.wait(t, C["tpair_ev_%d_%d" % (c, pp - 2)])
                    elif c >= 1:
                        sDV.wait(t, C["tpair_ev_%d_%d" % (c - 1, pp + 14)])
                    for hi, hh in enumerate((hA, hB)):
                        q = hi % 2
                        ob = hi * 64
                        t.matmul(py[ob:ob + 64, 0:256], x_tok[0][:, hh * 64:(hh + 1) * 64],
                                 Mt_sb[:, q * 384: q * 384 + 256],
                                 start=True, stop=False, tile_position=(0, ob))
                        t.matmul(py[ob:ob + 64, 128:256], x_tok[1][:, hh * 64:(hh + 1) * 64],
                                 Mt_sb[:, q * 384 + 256: q * 384 + 384],
                                 start=False, stop=False, tile_position=(0, ob))
                        t.matmul(py[ob:ob + 64, 0:256], carry_bf[:, hh * 64:(hh + 1) * 64],
                                 Mv_sb[:, q * CH:(q + 1) * CH],
                                 start=False, stop=True, tile_position=(0, ob))
                    if pp % 4 == 0:
                        if pp >= 4:
                            sDV.wait(t, C["carry_%d_%d" % (c, 2 * pp - 1)])
                        elif c >= 1:
                            sDV.wait(t, C["carry_%d_%d" % (c - 1, 31)])
                        if pp == 0:
                            sDV.wait(t, C["m1_ev_%d" % ((c // 2) * 34 + 33)])
                    for hi, hh in enumerate((hA, hB)):
                        q = hi % 2
                        slot = (hh % 8) * 64
                        t.matmul(psA[:, slot:slot + 64], Bmt[:, 0:128], xddb[:, q * 128: q * 128 + 64],
                                 start=(hi == 0), stop=False)
                        t.matmul(psA[:, slot:slot + 64], Bmt[:, 128:256],
                                 xddb[:, q * 128 + 64: q * 128 + 128], start=False, stop=(hi == 1))
                    C["hy_%d_%d" % (c, h)] = sPE.inc(t.last)
                    C["hy_%d_%d" % (c, h - 1)] = C["hy_%d_%d" % (c, h)]
                self.t(t_y)

                def v_cu(v, h=h, c=c):
                    hA, hB = h - 1, h
                    sPE.wait(v, C["hy_%d_%d" % (c, hB)])
                    for hh in (hA, hB):
                        slot = (hh % 8) * 64
                        if hh == hA and c == 0:
                            sAC.wait(v, C["cdb"])
                        v.scalar_tensor_tensor(carry[:, hh * 64:(hh + 1) * 64],
                                               carry[:, hh * 64:(hh + 1) * 64],
                                               cdb[:, hh * 8 + c: hh * 8 + c + 1],
                                               psA[:, slot:slot + 64], ALU.mult, ALU.add)
                        v.tensor_copy(carry_bf[:, hh * 64:(hh + 1) * 64], carry[:, hh * 64:(hh + 1) * 64])
                        C["carry_%d_%d" % (c, hh)] = sDV.inc(v.last)
                self.v(v_cu)

                def v_tp2(v, h=h, c=c, pp=pp):
                    sPE.wait(v, C["hy_%d_%d" % (c, h)])
                    sDV.wait(v, C["m2_u_%d_%d" % (c, pp // 8)])
                    if self.debug and c == 0 and pp == 0:
                        pass
                    py = ps_y[pp % 2]
                    v.scalar_tensor_tensor(ytmp[:, :], xpost[:, pp * CH:(pp + 1) * CH],
                                           self.vecs[:, pp:pp + 1], py[:, 0:256], ALU.mult, ALU.add)
                    v.scalar_tensor_tensor(tbc[:, pp * CH:(pp + 1) * CH], ytmp[:, :], 0.5,
                                           uall[:, pp * CH:(pp + 1) * CH], ALU.mult, ALU.mult)
                    C["tpair_ev_%d_%d" % (c, pp)] = sDV.inc(v.last)
                self.v(v_tp2)

                if self.debug >= 1 and c == 0 and pp == 0:
                    def gdbgy(g, c=c, pp=pp):
                        sDV.wait(g, C["tpair_ev_%d_%d" % (c, pp)])
                        self.sW.dma(g.dma_start(out=self.dbg["ytmp"][:, :], in_=ytmp[:, :]))
                        self.sW.dma(g.dma_start(out=self.dbg["uall"][:, :], in_=uall[:, :]))
                        self.sW.dma(g.dma_start(out=self.dbg["xpost"][:, :], in_=xpost[:, :]))
                        C["dbgy"] = self.sW.v
                    self.g(gdbgy)
                    def v_holdy(v):
                        self.sW.wait(v, C["dbgy"])
                        v.nop()
                        C["holdy"] = sDV.inc(v.last)
                    self.v(v_holdy)


        def a_hx_all(s, c=c):
            for h in range(32):
                par = h % 2
                sDV.wait(s, C["hl_%d_%d" % (c, h)])
                s.activation(Lsb[:, par * 384:(par + 1) * 384], Lsb[:, par * 384:(par + 1) * 384], AF.Exp)
                s.activation(Ecsb[:, par * CH:(par + 1) * CH], csb[:, par * CH:(par + 1) * CH], AF.Exp)
                s.last.then_inc(sAC.h, 1)
        self.s(a_hx_all)

        base_ms = sPE.v
        for pp16 in range(16):
            C["ms_mm_%d_%d" % (c, pp16)] = base_ms + pp16 + 1
        sPE.v = base_ms + 16

        def a_sq_all(s, c=c):
            for pp in range(16):
                sDV.wait(s, C["tpair_ev_%d_%d" % (c, pp)])
                if pp >= 2:
                    sPE.wait(s, C["ms_mm_%d_%d" % (c, pp - 2)])
                s.activation(sqt[:, (pp % 2) * CH:(pp % 2 + 1) * CH],
                             tbc[:, pp * CH:(pp + 1) * CH], AF.Square)
                s.last.then_inc(sAC.h, 1)
        self.s(a_sq_all)

        def t_ms_all(t, c=c):
            for pp in range(16):
                sAC.wait(t, C["sqt_%d_%d" % (c, pp)])
                t.matmul(psD[0:1, 0:256], self.ones_cb[:, :],
                         sqt[:, (pp % 2) * CH:(pp % 2 + 1) * CH],
                         start=(pp == 0), stop=(pp == 15))
                t.last.then_inc(sPE.h, 1)
        self.t(t_ms_all)

        def v_ms(v, c=c):
            sPE.wait(v, C["ms_mm_%d_%d" % (c, 15)])
            if c >= 1:
                sIn.wait(v, C["msw_%d" % (c - 1)])
            v.tensor_scalar_mul(self.msrow[0:1, :], psD[0:1, 0:256], 1.0 / DIN)
            C["ms_ev_%d" % c] = sDV.inc(v.last)
        self.v(v_ms)

        def s_msw(sp, c=c):
            sDV.wait(sp, C["ms_ev_%d" % c])
            C["msw_%d" % c] = sIn.dma(
                sp.dma_start(out=self.s_rows[3:4, c * CH:(c + 1) * CH], in_=self.msrow[0:1, :]))
        self.sp(s_msw)

        def s_td(sp, c=c):
            sDV.wait(sp, C["tpair_ev_%d_%d" % (c, 15)])
            C["m2_tdma_%d" % c] = sIn.dma(
                sp.dma_start(out=self.s_t[:, :, c * CH:(c + 1) * CH]
                             .rearrange("a p b -> p a b"),
                             in_=tbc[:, :].rearrange("p (a b) -> p a b", a=16)))
        self.sp(s_td)

        def v_tguard(v, c=c):
            sIn.wait(v, C["m2_tdma_%d" % c])
            v.nop()
            C["m2_tfree_%d" % c] = sDV.inc(v.last)
        self.v(v_tguard)

        if self.debug >= 1 and c == NCH - 1:
            def gdbg2(g, c=c):
                sDV.wait(g, C["cv_done_%d" % c])
                for t18 in range(16):
                    self.sW.dma(g.dma_start(out=self.dbg["xbc"][t18, :, :],
                                              in_=self.xpost[:, t18 * CH:(t18 + 1) * CH]))
                self.sW.dma(g.dma_start(out=self.dbg["xbc"][16, :, :], in_=self.bct[:, 0:256]))
                self.sW.dma(g.dma_start(out=self.dbg["xbc"][17, :, :], in_=self.bct[:, 256:512]))
            self.g(gdbg2)

    def phase_m3(self, AE):
        nc, C = self.nc, self.C
        sW, sIn, sSc, sPE, sDV, sAC = self.sW, self.sIn, self.sSc, self.sPE, self.sDV, self.sAC
        vecs = self.vecs

        w_out_sb = AE(nc.sbuf_tensor("w_out_sb", [128, 16 * 1024], BF16))
        tld = AE(nc.sbuf_tensor("tld", [128, 2 * 16 * 512], BF16))
        ynb = AE(nc.sbuf_tensor("ynb", [128, 2 * 16 * 512], BF16))
        mstg = AE(nc.sbuf_tensor("mstg", [128, 4 * 512], BF16))
        rmsr = AE(nc.sbuf_tensor("rmsr", [1, S], F32))
        rmsb = AE(nc.sbuf_tensor("rmsb", [128, 2 * 512], F32))
        ps_op = [AE(nc.psum_tensor("ps_opm%d" % i, [128, 512], F32)) for i in range(2)]

        def g_w(g):
            sPE.wait(g, C["m2_pe"])
            if self.debug:
                sW.wait(g, sW.v)
            for kt in range(16):
                sW.dma(g.dma_start(out=w_out_sb[:, kt * 1024:(kt + 1) * 1024],
                                   in_=self.w_outT[kt * 128:(kt + 1) * 128, :]))
            C["w_out_w"] = sW.v
        self.g(g_w)

        def s_msl(sp):
            sIn.wait(sp, C["m2_msw"])
            C["m3_msl"] = sIn.dma(sp.dma_start(out=rmsr[0:1, :], in_=self.s_rows[3:4, :]))
        self.sp(s_msl)

        def a_rms(s):
            sIn.wait(s, C["m3_msl"])
            s.activation(rmsr[0:1, :], rmsr[0:1, :], AF.Ln, bias=self.epsc[0:1, 0:1])
            s.activation(rmsr[0:1, :], rmsr[0:1, :], AF.Exp, scale=-0.5)
            C["m3_rms"] = sAC.inc(s.last)
        self.s(a_rms)

        def s_rmsw(sp):
            sAC.wait(sp, C["m3_rms"])
            C["m3_rmsw"] = sIn.dma(sp.dma_start(out=self.s_rows[4:5, :], in_=rmsr[0:1, :]))
        self.sp(s_rmsw)

        for tc in range(4):
            def s_tl(sp, tc=tc):
                sIn.wait(sp, C["m2_td"])
                sIn.wait(sp, C["m3_rmsw"])
                if tc >= 2:
                    sDV.wait(sp, C["m3_yn_%d" % (tc - 2)])
                C["m3_tl_%d" % tc] = sIn.dma(
                    sp.dma_start(out=tld[:, (tc % 2) * 16 * 512:((tc % 2) + 1) * 16 * 512]
                                 .rearrange("p (a b) -> p a b", a=16),
                                 in_=self.s_t[:, :, tc * 512:(tc + 1) * 512]
                                 .rearrange("a p b -> p a b")))
                C["m3_rb_%d" % tc] = sSc.dma(
                    sp.dma_start(out=rmsb[:, (tc % 2) * 512:(tc % 2) * 512 + 512],
                                 in_=self.s_rows[4:5, tc * 512:(tc + 1) * 512].broadcast_to([128, 512])))
            self.sp(s_tl)

            def v_yn(v, tc=tc):
                sSc.wait(v, C["m3_rb_%d" % tc])
                sIn.wait(v, C["m3_tl_%d" % tc])
                off = (tc % 2) * 16 * 512
                for kt in range(16):
                    v.scalar_tensor_tensor(ynb[:, off + kt * 512: off + (kt + 1) * 512],
                                           tld[:, off + kt * 512: off + (kt + 1) * 512],
                                           vecs[:, 16 + kt:17 + kt],
                                           rmsb[:, (tc % 2) * 512:(tc % 2) * 512 + 512],
                                           ALU.mult, ALU.mult)
                C["m3_yn_%d" % tc] = sDV.inc(v.last)
            self.v(v_yn)

            for mt in range(8):
                def t_op(t, tc=tc, mt=mt):
                    i = tc * 8 + mt
                    if i == 0:
                        sW.wait(t, C["w_out_w"])
                        sDV.wait(t, C["m2_dv"])
                        sAC.wait(t, C["m2_ac"])
                    sDV.wait(t, C["m3_yn_%d" % tc])
                    if i >= 2:
                        sDV.wait(t, C["m3_ev_%d" % (i - 2)])
                    ps = ps_op[i % 2]
                    off = (tc % 2) * 16 * 512
                    for kt in range(16):
                        t.matmul(ps[:, :], w_out_sb[:, kt * 1024 + mt * 128: kt * 1024 + (mt + 1) * 128],
                                 ynb[:, off + kt * 512: off + (kt + 1) * 512],
                                 start=(kt == 0), stop=(kt == 15))
                    C["m3_mm_%d" % i] = sPE.inc(t.last)
                self.t(t_op)

                def v_op(v, tc=tc, mt=mt):
                    i = tc * 8 + mt
                    sPE.wait(v, C["m3_mm_%d" % i])
                    if i >= 4:
                        sIn.wait(v, C["m3_md_%d" % (i - 4)])
                    v.tensor_copy(mstg[:, (i % 4) * 512:(i % 4) * 512 + 512], ps_op[i % 2][:, :])
                    C["m3_ev_%d" % i] = sDV.inc(v.last)
                self.v(v_op)

                def s_op(sp, tc=tc, mt=mt):
                    i = tc * 8 + mt
                    sDV.wait(sp, C["m3_ev_%d" % i])
                    C["m3_md_%d" % i] = sIn.dma(
                        sp.dma_start(out=self.s_m[mt, :, tc * 512:(tc + 1) * 512],
                                     in_=mstg[:, (i % 4) * 512:(i % 4) * 512 + 512]))
                self.sp(s_op)
        C["m3_md"] = C["m3_md_31"]
        C["m3_dv"] = sDV.v
        C["m3_pe"] = sPE.v

        if self.debug >= 1:
            def gdbg3(g):
                sIn.wait(g, C["m3_md"])
                for mt in range(8):
                    self.sW.dma(g.dma_start(out=self.dbg["m"][mt, :, :], in_=self.s_m[mt, :, :]))
                self.sW.dma(g.dma_start(out=self.dbg["rstd"][2:3, :], in_=self.s_rows[3:4, :]))
                self.sW.dma(g.dma_start(out=self.dbg["rstd"][3:4, :], in_=self.s_rows[4:5, :]))
                for kt in range(16):
                    self.sW.dma(g.dma_start(out=self.dbg["t"][kt, :, :], in_=self.s_t[kt, :, :]))
            self.g(gdbg3)

    def phase_attn(self, AE):
        nc, C = self.nc, self.C
        sW, sIn, sSc, sPE, sDV, sAC = self.sW, self.sIn, self.sSc, self.sPE, self.sDV, self.sAC
        xn, vecs = self.xn, self.vecs

        qT = AE(nc.sbuf_tensor("qT", [128, 8 * S], BF16))
        kT = AE(nc.sbuf_tensor("kT", [128, 8 * S], BF16))
        v_store = AE(nc.sbuf_tensor("v_store", [128, 16 * 1056], BF16))
        nbb = AE(nc.sbuf_tensor("nbb", [128, 32], F32))

        with ExitStack() as sub:
            SE = sub.enter_context
            xn2 = SE(nc.sbuf_tensor("xn2", [128, 8 * S], BF16))
            bmax = SE(nc.sbuf_tensor("bmax", [4, 24], F32))
            u2b = SE(nc.sbuf_tensor("u2b", [128, 512], F32))
            r2b = SE(nc.sbuf_tensor("r2b", [128, 2 * 512], F32))
            wv_sb = SE(nc.sbuf_tensor("wv_sb", [128, 8 * 1024], BF16))
            wcol = SE(nc.sbuf_tensor("wcol", [128, 4 * 1024], BF16))
            sqs = SE(nc.sbuf_tensor("sqs", [128, 2 * 512], F32))
            ps_qk = [SE(nc.psum_tensor("ps_qk%d" % i, [128, 512], F32)) for i in range(4)]

            def g_wv(g):
                sPE.wait(g, C["m3_pe"])
                if self.debug:
                    sW.wait(g, sW.v)
                for kt in range(8):
                    sW.dma(g.dma_start(out=wv_sb[:, kt * 1024:(kt + 1) * 1024],
                                       in_=self.wv[kt * 128:(kt + 1) * 128, :]))
                C["wv_w"] = sW.v
            self.g(g_wv)

            for tc in range(4):
                def s_r2(sp, tc=tc):
                    sIn.wait(sp, C["p0_rw2"])
                    if tc >= 2:
                        sDV.wait(sp, C["xn2_%d" % (tc - 2)])
                    C["l2bc_%d" % tc] = sSc.dma(
                        sp.dma_start(out=r2b[:, (tc % 2) * 512:(tc % 2) * 512 + 512],
                                     in_=self.s_rows[2:3, tc * 512:(tc + 1) * 512].broadcast_to([128, 512])))
                self.sp(s_r2)

                def v_l2(v, tc=tc):
                    sSc.wait(v, C["l2bc_%d" % tc])
                    if tc == 0:
                        sDV.wait(v, C["m3_dv"])
                    for kt in range(8):
                        sl = slice(kt * S + tc * 512, kt * S + tc * 512 + 512)
                        v.scalar_tensor_tensor(u2b[:, :], xn[:, sl], vecs[:, 32 + kt:33 + kt],
                                               r2b[:, (tc % 2) * 512:(tc % 2) * 512 + 512],
                                               ALU.mult, ALU.mult)
                        v.tensor_scalar_add(xn2[:, sl], u2b[:, :], vecs[:, 40 + kt:41 + kt])
                    C["xn2_%d" % tc] = sDV.inc(v.last)
                self.v(v_l2)
            C["xn2_all"] = C["xn2_3"]

            if self.debug >= 1:
                def gdbgx2(g):
                    sDV.wait(g, C["xn2_all"])
                    for kt in range(8):
                        self.sW.dma(g.dma_start(out=self.dbg["xn2"][kt, :, :],
                                                  in_=xn2[:, kt * S:(kt + 1) * S]))
                self.g(gdbgx2)

            for pj, wt, dest in [(0, self.wq, qT), (1, self.wk, kT)]:
                for mt in range(8):
                    i = pj * 8 + mt
                    def g_slab(g, i=i, wt=wt, mt=mt):
                        if i >= 4:
                            sPE.wait(g, C["qk_mm_%d_%d" % (i - 4, 3)])
                        C["wqk_%d" % i] = sW.dma(g.dma_start(
                            out=wcol[:, (i % 4) * 1024:(i % 4) * 1024 + 1024]
                                .rearrange("p (kt j) -> p kt j", kt=8),
                            in_=wt[:, mt * 128:(mt + 1) * 128].rearrange("(kt p) j -> p kt j", p=128)))
                    self.g(g_slab)

                    for tc in range(4):
                        def t_qk(t, i=i, mt=mt, tc=tc):
                            gi = i * 4 + tc
                            sW.wait(t, C["wqk_%d" % i])
                            sDV.wait(t, C["xn2_%d" % tc])
                            if gi >= 4:
                                sDV.wait(t, C["qk_ev_%d_%d" % ((gi - 4) // 4, (gi - 4) % 4)])
                            ps = ps_qk[gi % 4]
                            for kt in range(8):
                                t.matmul(ps[:, :],
                                         wcol[:, (i % 4) * 1024 + kt * 128: (i % 4) * 1024 + (kt + 1) * 128],
                                         xn2[:, kt * S + tc * 512: kt * S + tc * 512 + 512],
                                         start=(kt == 0), stop=(kt == 7))
                            C["qk_mm_%d_%d" % (i, tc)] = sPE.inc(t.last)
                        self.t(t_qk)

                        def v_qk(v, i=i, mt=mt, tc=tc, dest=dest):
                            gi = i * 4 + tc
                            sPE.wait(v, C["qk_mm_%d_%d" % (i, tc)])
                            v.tensor_copy(dest[:, mt * S + tc * 512: mt * S + tc * 512 + 512],
                                          ps_qk[gi % 4][:, :])
                            C["qk_ev_%d_%d" % (i, tc)] = sDV.inc(v.last)
                        self.v(v_qk)

            for ttg in range(16):
                def v_vs(v, ttg=ttg):
                    base = v_store[:, ttg * 1056:(ttg + 1) * 1056].rearrange("p (h x) -> p h x", x=33)
                    v.memset(base[:, :, 32:33], 1.0)
                    C["vs1_%d" % ttg] = sDV.inc(v.last)
                self.v(v_vs)
                tc, tt = ttg // 4, ttg % 4
                for half in range(2):
                    gi = ttg * 2 + half
                    def t_v(t, tc=tc, tt=tt, half=half, ttg=ttg, gi=gi):
                        if gi == 0:
                            sW.wait(t, C["wv_w"])
                        sDV.wait(t, C["xn2_%d" % tc])
                        if gi >= 4:
                            sDV.wait(t, C["v_ev_%d" % (gi - 4)])
                        else:
                            sDV.wait(t, C["qk_ev_%d_%d" % (15, gi)])
                        ps = ps_qk[gi % 4]
                        for kt in range(8):
                            t.matmul(ps[:, :],
                                     xn2[:, kt * S + tc * 512 + tt * 128: kt * S + tc * 512 + (tt + 1) * 128],
                                     wv_sb[:, kt * 1024 + half * 512: kt * 1024 + (half + 1) * 512],
                                     start=(kt == 0), stop=(kt == 7))
                        C["v_mm_%d" % gi] = sPE.inc(t.last)
                    self.t(t_v)

                    def v_v(v, half=half, ttg=ttg, gi=gi):
                        sPE.wait(v, C["v_mm_%d" % gi])
                        sDV.wait(v, C["vs1_%d" % ttg])
                        base = v_store[:, ttg * 1056:(ttg + 1) * 1056].rearrange("p (h x) -> p h x", x=33)
                        v.tensor_copy(base[:, half * 16:(half + 1) * 16, 0:32],
                                      ps_qk[gi % 4][:, :].rearrange("p (h x) -> p h x", x=32))
                        C["v_ev_%d" % gi] = sDV.inc(v.last)
                    self.v(v_v)

            def v_bm0(v):
                v.memset(bmax[:, :], 0.0)
                C["bm0"] = sDV.inc(v.last)
            self.v(v_bm0)
            for col, src in [(0, qT), (1, kT)]:
                for dt in range(8):
                    for tc in range(4):
                        sg = (col * 8 + dt) * 4 + tc
                        def a_st(s, col=col, src=src, dt=dt, tc=tc, sg=sg):
                            if sg == 0:
                                sDV.wait(s, C["qk_ev_%d_%d" % (15, 3)])
                                sDV.wait(s, C["v_ev_31"])
                            elif sg >= 2:
                                sPE.wait(s, C["st_mm_%d" % (sg - 2)])
                            s.activation(sqs[:, (sg % 2) * 512:(sg % 2) * 512 + 512],
                                         src[:, dt * S + tc * 512: dt * S + tc * 512 + 512],
                                         AF.Square)
                            C["st_sq_%d" % sg] = sAC.inc(s.last)
                        self.s(a_st)

                        def t_st(t, col=col, dt=dt, tc=tc, sg=sg):
                            sAC.wait(t, C["st_sq_%d" % sg])
                            if sg == 0:
                                sDV.wait(t, C["v_ev_31"])
                            elif sg >= 4:
                                sDV.wait(t, C["st_ev_%d" % (sg - 4)])
                            t.matmul(ps_qk[sg % 4][0:4, 0:512], self.cb4T[:, :],
                                     sqs[:, (sg % 2) * 512:(sg % 2) * 512 + 512],
                                     start=True, stop=True)
                            C["st_mm_%d" % sg] = sPE.inc(t.last)
                        self.t(t_st)

                        def v_st(v, col=col, dt=dt, tc=tc, sg=sg):
                            sPE.wait(v, C["st_mm_%d" % sg])
                            if sg == 0:
                                sDV.wait(v, C["bm0"])
                            v.reduce_max(bmax[0:4, 16:17], ps_qk[sg % 4][0:4, 0:512], AXX)
                            v.tensor_max(bmax[0:4, col * 8 + dt: col * 8 + dt + 1],
                                         bmax[0:4, col * 8 + dt: col * 8 + dt + 1], bmax[0:4, 16:17])
                            C["st_ev_%d" % sg] = sDV.inc(v.last)
                        self.v(v_st)

            def v_b1(v):
                v.tensor_mul(bmax[0:4, 8:16], bmax[0:4, 0:8], bmax[0:4, 8:16])
                C["b_u"] = sDV.inc(v.last)
            self.v(v_b1)

            def a_b(s):
                sDV.wait(s, C["b_u"])
                s.activation(bmax[0:4, 8:16], bmax[0:4, 8:16], AF.Ln)
                s.activation(bmax[0:4, 8:16], bmax[0:4, 8:16], AF.Exp, scale=0.5)
                C["b_sq"] = sAC.inc(s.last)
            self.s(a_b)

            def v_b2(v):
                sAC.wait(v, C["b_sq"])
                v.tensor_scalar_mul(bmax[0:4, 8:16], bmax[0:4, 8:16], -RSQ32)
                C["b_nb"] = sDV.inc(v.last)
            self.v(v_b2)

            def s_nbw(sp):
                sDV.wait(sp, C["b_nb"])
                C["nb_w"] = sIn.dma(sp.dma_start(
                    out=self.s_rows[5:6, 0:32].rearrange("o (dt j) -> (o j) dt", j=4),
                    in_=bmax[0:4, 8:16], allow_slow_non_contiguous=True))
            self.sp(s_nbw)

            def s_nbl(sp):
                sIn.wait(sp, C["nb_w"])
                C["nb_l"] = sSc.dma(sp.dma_start(
                    out=nbb[:, :], in_=self.s_rows[5:6, 0:32].broadcast_to([128, 32])))
            self.sp(s_nbl)
            C["qkv_dv"] = sDV.v

        sub2 = ExitStack()
        attnout = sub2.enter_context(nc.sbuf_tensor("attnout", [128, 8 * S], BF16))
        with ExitStack() as sub:
            SE = sub.enter_context
            pstg = SE(nc.sbuf_tensor("pstg", [128, 4 * 1024], BF16))
            sstg = SE(nc.sbuf_tensor("sstg", [33, 4 * 1024], F32))
            pT = SE(nc.sbuf_tensor("pT", [128, 4 * 1024], BF16))
            ps_s = [SE(nc.psum_tensor("ps_s%d" % i, [128, 1024], F32)) for i in range(2)]
            ps_pv = [SE(nc.psum_tensor("ps_pv%d" % i, [33, 1024], F32)) for i in range(2)]

            for pp in range(16):
                dt, u = pp // 2, pp % 2
                hA, hB = 4 * dt + 2 * u, 4 * dt + 2 * u + 1
                for qc in range(2):
                    gi = pp * 2 + qc
                    base_fx = sAC.v
                    for kt in range(16):
                        C["f_x_%d_%d" % (gi, kt)] = base_fx + kt + 1
                    sAC.v = base_fx + 16
                    for kt in range(16):
                        def t_sv(t, dt=dt, u=u, hA=hA, hB=hB, qc=qc, kt=kt, gi=gi):
                            if gi == 0 and kt == 0:
                                sDV.wait(t, C["qkv_dv"])
                                sSc.wait(t, C["nb_l"])
                            if kt > 0:
                                sAC.wait(t, C["f_x_%d_%d" % (gi, kt - 1)])
                            elif gi > 0:
                                sAC.wait(t, C["f_x_%d_%d" % (gi - 1, 15)])
                            for hi in range(2):
                                strip = 2 * u + hi
                                for nck in range(2):
                                    t.matmul(ps_s[hi][:, nck * 512:(nck + 1) * 512],
                                             kT[32 * strip:32 * strip + 32,
                                                dt * S + kt * 128: dt * S + (kt + 1) * 128],
                                             qT[32 * strip:32 * strip + 32,
                                                dt * S + qc * 1024 + nck * 512:
                                                dt * S + qc * 1024 + (nck + 1) * 512],
                                             start=True, stop=True, tile_position=(32 * strip, 0))
                            C["f_s_%d_%d" % (gi, kt)] = sPE.inc(t.last)
                            if kt > 0:
                                if kt - 1 == 0 and gi >= 1:
                                    sDV.wait(t, C["f_ev_%d" % (gi - 1)])
                                for hi, h in enumerate((hA, hB)):
                                    for nck in range(2):
                                        t.matmul(ps_pv[hi][:, nck * 512:(nck + 1) * 512],
                                                 v_store[:, (kt - 1) * 1056 + 33 * h:
                                                         (kt - 1) * 1056 + 33 * h + 33],
                                                 pT[:, (hi * 2 + ((kt - 1) % 2)) * 1024 + nck * 512:
                                                    (hi * 2 + ((kt - 1) % 2)) * 1024 + (nck + 1) * 512],
                                                 start=(kt - 1 == 0), stop=False)
                                C["f_pv_%d_%d" % (gi, kt - 1)] = sPE.inc(t.last)
                        self.t(t_sv)

                    def a_x_all(s, hA=hA, hB=hB, qc=qc, gi=gi, base_fx=base_fx):
                        for kt in range(16):
                            sPE.wait(s, C["f_s_%d_%d" % (gi, kt)])
                            if kt >= 2:
                                sPE.wait(s, C["f_pv_%d_%d" % (gi, kt - 2)])
                            elif gi >= 1:
                                if kt == 0:
                                    sPE.wait(s, C["f_pv_%d_%d" % (gi - 1, 14)])
                                else:
                                    sPE.wait(s, C["f_pvl_%d" % (gi - 1)])
                            for hi, h in enumerate((hA, hB)):
                                s.activation(pT[:, (hi * 2 + kt % 2) * 1024:(hi * 2 + kt % 2 + 1) * 1024],
                                             ps_s[hi][:, :], AF.Exp, scale=RSQ32, bias=nbb[:, h:h + 1])
                            s.last.then_inc(sAC.h, 1)
                    self.s(a_x_all)

                    def t_pvl(t, hA=hA, hB=hB, gi=gi):
                        sAC.wait(t, C["f_x_%d_%d" % (gi, 15)])
                        for hi, h in enumerate((hA, hB)):
                            for nck in range(2):
                                t.matmul(ps_pv[hi][:, nck * 512:(nck + 1) * 512],
                                         v_store[:, 15 * 1056 + 33 * h: 15 * 1056 + 33 * h + 33],
                                         pT[:, (hi * 2 + 1) * 1024 + nck * 512:
                                            (hi * 2 + 1) * 1024 + (nck + 1) * 512],
                                         start=False, stop=True)
                        C["f_pvl_%d" % gi] = sPE.inc(t.last)
                    self.t(t_pvl)

                    def v_ev(v, dt=dt, u=u, hA=hA, hB=hB, qc=qc, gi=gi):
                        sPE.wait(v, C["f_pvl_%d" % gi])
                        if gi >= 2:
                            sSc.wait(v, C["f_evd_%d" % (gi - 2)])
                        for hi, h in enumerate((hA, hB)):
                            sl = ((gi % 2) * 2 + hi) * 1024
                            v.tensor_copy(pstg[0:32, sl:sl + 1024], ps_pv[hi][0:32, :])
                            v.tensor_copy(sstg[32:33, sl:sl + 1024], ps_pv[hi][32:33, :])
                        C["f_ev_%d" % gi] = sDV.inc(v.last)
                    self.v(v_ev)

                    def s_ev(sp, dt=dt, u=u, hA=hA, hB=hB, qc=qc, gi=gi):
                        sDV.wait(sp, C["f_ev_%d" % gi])
                        for hi, h in enumerate((hA, hB)):
                            strip = 2 * u + hi
                            sl = ((gi % 2) * 2 + hi) * 1024
                            sSc.dma(sp.dma_start(
                                out=attnout[32 * strip:32 * strip + 32,
                                            dt * S + qc * 1024: dt * S + (qc + 1) * 1024],
                                in_=pstg[0:32, sl:sl + 1024]))
                            sSc.dma(sp.dma_start(
                                out=self.s_sums[h:h + 1, qc * 1024:(qc + 1) * 1024],
                                in_=sstg[32:33, sl:sl + 1024]))
                        C["f_evd_%d" % gi] = sSc.v
                    self.s(s_ev)
            C["f_attn_dv"] = sDV.v
            C["f_attn_dma"] = sSc.v

        sub3 = ExitStack()
        sums_sb = sub3.enter_context(nc.sbuf_tensor("sums_sb", [32, S], F32))
        ostg = sub3.enter_context(nc.sbuf_tensor("ostg", [128, 4 * 512], BF16))

        def s_rsl(sp):
            sSc.wait(sp, C["f_evd_31"])
            C["f_rsl"] = sIn.dma(sp.dma_start(out=sums_sb[:, :], in_=self.s_sums[:, :]))
        self.sp(s_rsl)

        def v_rc(v):
            sIn.wait(v, C["f_rsl"])
            v.reciprocal(sums_sb[:, :], sums_sb[:, :])
            C["f_rs"] = sDV.inc(v.last)
        self.v(v_rc)

        def s_rsw(sp):
            sDV.wait(sp, C["f_rs"])
            C["f_rsw"] = sIn.dma(sp.dma_start(out=self.s_sums[:, :], in_=sums_sb[:, :]))
        self.sp(s_rsw)

        with ExitStack() as sub:
            SE = sub.enter_context
            wcol2 = SE(nc.sbuf_tensor("wcol2", [128, 2 * 1024], BF16))
            rbb = SE(nc.sbuf_tensor("rbb", [128, 2 * 512], F32))
            ps_o = [SE(nc.psum_tensor("ps_o%d" % i, [128, 512], F32)) for i in range(2)]

            for dt in range(8):
                for tc in range(4):
                    gi = dt * 4 + tc
                    def s_rb(sp, dt=dt, tc=tc, gi=gi):
                        sIn.wait(sp, C["f_rsw"])
                        if gi >= 2:
                            sDV.wait(sp, C["f_nm_%d" % (gi - 2)])
                        for h in range(4):
                            C["f_rb_%d" % gi] = sSc.dma(sp.dma_start(
                                out=rbb[h * 32:(h + 1) * 32, (gi % 2) * 512:(gi % 2) * 512 + 512],
                                in_=self.s_sums[4 * dt + h:4 * dt + h + 1, tc * 512:(tc + 1) * 512]
                                    .broadcast_to([32, 512])))
                    self.sp(s_rb)

                    def v_nm(v, dt=dt, tc=tc, gi=gi):
                        sSc.wait(v, C["f_rb_%d" % gi])
                        sl = slice(dt * S + tc * 512, dt * S + tc * 512 + 512)
                        v.tensor_mul(attnout[:, sl], attnout[:, sl],
                                     rbb[:, (gi % 2) * 512:(gi % 2) * 512 + 512])
                        C["f_nm_%d" % gi] = sDV.inc(v.last)
                    self.v(v_nm)
            C["f_nm"] = sDV.v

            if self.debug >= 1:
                def gdbga(g):
                    sDV.wait(g, C["f_nm"])
                    for dt in range(8):
                        self.sW.dma(g.dma_start(out=self.dbg["ao"][dt, :, :],
                                                  in_=attnout[:, dt * S:(dt + 1) * S]))
                self.g(gdbga)

            for mt in range(8):
                def g_wo(g, mt=mt):
                    if mt >= 2:
                        sPE.wait(g, C["o_mm_%d_%d" % (mt - 2, 3)])
                    else:
                        sPE.wait(g, C["f_pvl_31"])
                    C["wo_%d" % mt] = sW.dma(g.dma_start(
                        out=wcol2[:, (mt % 2) * 1024:(mt % 2) * 1024 + 1024]
                            .rearrange("p (kt j) -> p kt j", kt=8),
                        in_=self.wo[:, mt * 128:(mt + 1) * 128].rearrange("(kt p) j -> p kt j", p=128)))
                self.g(g_wo)

                for tc in range(4):
                    gi = mt * 4 + tc
                    def t_o(t, mt=mt, tc=tc, gi=gi):
                        sW.wait(t, C["wo_%d" % mt])
                        if gi == 0:
                            sDV.wait(t, C["f_nm"])
                        if gi >= 2:
                            sDV.wait(t, C["o_ev_%d" % (gi - 2)])
                        ps = ps_o[gi % 2]
                        for kt in range(8):
                            t.matmul(ps[:, :],
                                     wcol2[:, (mt % 2) * 1024 + kt * 128: (mt % 2) * 1024 + (kt + 1) * 128],
                                     attnout[:, kt * S + tc * 512: kt * S + tc * 512 + 512],
                                     start=(kt == 0), stop=(kt == 7))
                        C["o_mm_%d_%d" % (mt, tc)] = sPE.inc(t.last)
                    self.t(t_o)

                    def v_o(v, mt=mt, tc=tc, gi=gi):
                        sPE.wait(v, C["o_mm_%d_%d" % (mt, tc)])
                        if gi >= 4:
                            sIn.wait(v, C["o_dma_%d" % (gi - 4)])
                        v.tensor_add(ostg[:, (gi % 4) * 512:(gi % 4) * 512 + 512],
                                     ps_o[gi % 2][:, :], xn[:, mt * S + tc * 512: mt * S + tc * 512 + 512])
                        C["o_ev_%d" % gi] = sDV.inc(v.last)
                    self.v(v_o)

                    def s_so(sp, mt=mt, tc=tc, gi=gi):
                        sDV.wait(sp, C["o_ev_%d" % gi])
                        C["o_dma_%d" % gi] = sIn.dma(
                            sp.dma_start(out=self.s_o[mt, :, tc * 512:(tc + 1) * 512],
                                         in_=ostg[:, (gi % 4) * 512:(gi % 4) * 512 + 512]))
                    self.sp(s_so)
            C["o_done"] = C["o_dma_31"]
            C["f_dv"] = sDV.v
            C["f_pe"] = sPE.v

            if self.debug >= 1:
                def gdbgo(g):
                    sIn.wait(g, C["o_done"])
                    for mt in range(8):
                        self.sW.dma(g.dma_start(out=self.dbg["o"][mt, :, :], in_=self.s_o[mt, :, :]))
                self.g(gdbgo)
        sub3.close()
        sub2.close()

    def phase_moe_p5(self, AE):
        nc, C = self.nc, self.C
        sW, sIn, sPE, sDV, sAC, sOut = self.sW, self.sIn, self.sPE, self.sDV, self.sAC, self.sOut

        osl = AE(nc.sbuf_tensor("osl", [128, 8 * 1024], BF16))
        w1s = AE(nc.sbuf_tensor("w1s", [128, 8 * 2048], BF16))
        w2s = AE(nc.sbuf_tensor("w2s", [128, 16 * 1024], BF16))
        h_sb = AE(nc.sbuf_tensor("h_sb", [128, 16 * 1024], BF16))
        y_acc = AE(nc.sbuf_tensor("y_acc", [128, 8 * 1024], F32))
        xstg = AE(nc.sbuf_tensor("xstg", [128, 2 * 512], F32))
        mstg2 = AE(nc.sbuf_tensor("mstg2", [128, 2 * 512], BF16))
        fstg = AE(nc.sbuf_tensor("fstg", [128, 2 * 512], F32))
        ps_h = [AE(nc.psum_tensor("ps_h%d" % i, [128, 512], F32)) for i in range(2)]
        ps_yy = [AE(nc.psum_tensor("ps_yy%d" % i, [128, 512], F32)) for i in range(2)]

        for e in range(2):
            tok0 = e * 1024

            def s_ol(sp, e=e, tok0=tok0):
                sIn.wait(sp, C["o_done"])
                if e == 0:
                    sPE.wait(sp, C["f_pe"])
                else:
                    sPE.wait(sp, C["moe_hlast_%d_%d" % (0, 1)])
                C["osl_%d" % e] = sIn.dma(
                    sp.dma_start(out=osl[:, :].rearrange("p (a b) -> p a b", a=8),
                                 in_=self.s_o[:, :, tok0:tok0 + 1024]
                                 .rearrange("a p b -> p a b")))
            self.sp(s_ol)

            for fh in range(2):
                def g_w1(g, e=e, fh=fh):
                    if e == 0 and fh == 0:
                        sPE.wait(g, C["f_pe"])
                        if self.debug:
                            sW.wait(g, sW.v)
                    elif fh == 1:
                        sPE.wait(g, C["moe_hlast_%d_%d" % (e, 0)])
                    else:
                        sPE.wait(g, C["moe_hlast_%d_%d" % (e - 1, 1)])
                    for kt in range(8):
                        sW.dma(g.dma_start(out=w1s[:, kt * 2048:(kt + 1) * 2048],
                                           in_=self.w1[e, kt * 128:(kt + 1) * 128,
                                                       fh * 2048:(fh + 1) * 2048]))
                    C["w1_%d_%d" % (e, fh)] = sW.v
                self.g(g_w1)

                def g_w2(g, e=e, fh=fh):
                    if e == 0 and fh == 0:
                        sPE.wait(g, C["f_pe"])
                    elif fh == 1:
                        sPE.wait(g, C["moe_ylast_%d_%d" % (e, 0)])
                    else:
                        sPE.wait(g, C["moe_ylast_%d_%d" % (e - 1, 1)])
                    for ft in range(16):
                        sW.dma(g.dma_start(out=w2s[:, ft * 1024:(ft + 1) * 1024],
                                           in_=self.w2[e, fh * 2048 + ft * 128:
                                                       fh * 2048 + (ft + 1) * 128, :]))
                    C["w2_%d_%d" % (e, fh)] = sW.v
                self.g(g_w2)

                base_g = sAC.v
                for gi in range(32):
                    C["moe_g_%d_%d_%d" % (e, fh, gi)] = base_g + gi + 1
                sAC.v = base_g + 32
                for ft in range(16):
                    for tt in range(2):
                        gi = ft * 2 + tt

                        def t_h(t, e=e, fh=fh, ft=ft, tt=tt, gi=gi):
                            if gi == 0:
                                sW.wait(t, C["w1_%d_%d" % (e, fh)])
                                sIn.wait(t, C["osl_%d" % e])
                            if gi >= 2:
                                sAC.wait(t, C["moe_g_%d_%d_%d" % (e, fh, gi - 2)])
                            ps = ps_h[gi % 2]
                            for kt in range(8):
                                t.matmul(ps[:, :],
                                         w1s[:, kt * 2048 + ft * 128: kt * 2048 + (ft + 1) * 128],
                                         osl[:, kt * 1024 + tt * 512: kt * 1024 + (tt + 1) * 512],
                                         start=(kt == 0), stop=(kt == 7))
                            C["moe_h_%d_%d_%d" % (e, fh, gi)] = sPE.inc(t.last)
                        self.t(t_h)

                def a_g_all(s, e=e, fh=fh):
                    for ft in range(16):
                        for tt in range(2):
                            gi = ft * 2 + tt
                            sPE.wait(s, C["moe_h_%d_%d_%d" % (e, fh, gi)])
                            s.activation(h_sb[:, ft * 1024 + tt * 512: ft * 1024 + (tt + 1) * 512],
                                         ps_h[gi % 2][:, :], AF.Gelu,
                                         bias=self.b1_sb[:, e * 32 + fh * 16 + ft:
                                                         e * 32 + fh * 16 + ft + 1])
                            s.last.then_inc(sAC.h, 1)
                self.s(a_g_all)
                C["moe_hlast_%d_%d" % (e, fh)] = C["moe_h_%d_%d_%d" % (e, fh, 31)]

                for dt in range(8):
                    for tt in range(2):
                        gi = dt * 2 + tt

                        def t_yy(t, e=e, fh=fh, dt=dt, tt=tt, gi=gi):
                            if gi == 0:
                                sW.wait(t, C["w2_%d_%d" % (e, fh)])
                            sAC.wait(t, C["moe_g_%d_%d_%d" % (e, fh, 31)])
                            if gi >= 2:
                                sDV.wait(t, C["moe_ye_%d_%d_%d" % (e, fh, gi - 2)])
                            ps = ps_yy[gi % 2]
                            for ft in range(16):
                                t.matmul(ps[:, :],
                                         w2s[:, ft * 1024 + dt * 128: ft * 1024 + (dt + 1) * 128],
                                         h_sb[:, ft * 1024 + tt * 512: ft * 1024 + (tt + 1) * 512],
                                         start=(ft == 0), stop=(ft == 15))
                            C["moe_y_%d_%d_%d" % (e, fh, gi)] = sPE.inc(t.last)
                        self.t(t_yy)

                        def v_yy(v, e=e, fh=fh, dt=dt, tt=tt, gi=gi):
                            sPE.wait(v, C["moe_y_%d_%d_%d" % (e, fh, gi)])
                            sl = slice(dt * 1024 + tt * 512, dt * 1024 + (tt + 1) * 512)
                            if fh == 0:
                                v.tensor_copy(y_acc[:, sl], ps_yy[gi % 2][:, :])
                            else:
                                v.tensor_add(y_acc[:, sl], y_acc[:, sl], ps_yy[gi % 2][:, :])
                            C["moe_ye_%d_%d_%d" % (e, fh, gi)] = sDV.inc(v.last)
                        self.v(v_yy)
                C["moe_ylast_%d_%d" % (e, fh)] = C["moe_y_%d_%d_%d" % (e, fh, 15)]

            for dt in range(8):
                for tt in range(2):
                    gi = dt * 2 + tt

                    def s_xl(sp, e=e, dt=dt, tt=tt, gi=gi, tok0=tok0):
                        if e > 0 or gi >= 2:
                            key = "p5_ev_%d_%d" % ((e, gi - 2) if gi >= 2 else (e - 1, 14 + gi))
                            sDV.wait(sp, C[key])
                        sl = slice(tok0 + tt * 512, tok0 + (tt + 1) * 512)
                        C["p5_x_%d_%d" % (e, gi)] = sIn.dma(
                            sp.dma_start(out=xstg[:, (gi % 2) * 512:(gi % 2) * 512 + 512],
                                         in_=self.xT[dt * 128:(dt + 1) * 128, sl]))
                        C["p5_m_%d_%d" % (e, gi)] = sIn.dma(
                            sp.dma_start(out=mstg2[:, (gi % 2) * 512:(gi % 2) * 512 + 512],
                                         in_=self.s_m[dt, :, sl]))
                    self.sp(s_xl)

                    def v_p5(v, e=e, dt=dt, tt=tt, gi=gi):
                        sIn.wait(v, C["p5_m_%d_%d" % (e, gi)])
                        sDV.wait(v, C["moe_ye_%d_%d_%d" % (e, 1, gi)])
                        if e > 0 or gi >= 2:
                            key = "p5_od_%d_%d" % ((e, gi - 2) if gi >= 2 else (e - 1, 14 + gi))
                            sOut.wait(v, C[key])
                        sl = slice(dt * 1024 + tt * 512, dt * 1024 + (tt + 1) * 512)
                        fs = fstg[:, (gi % 2) * 512:(gi % 2) * 512 + 512]
                        v.tensor_scalar_add(fs, y_acc[:, sl], self.b2_sb[:, e * 8 + dt: e * 8 + dt + 1])
                        v.tensor_add(fs, fs, mstg2[:, (gi % 2) * 512:(gi % 2) * 512 + 512])
                        v.tensor_add(fs, fs, xstg[:, (gi % 2) * 512:(gi % 2) * 512 + 512])
                        C["p5_ev_%d_%d" % (e, gi)] = sDV.inc(v.last)
                    self.v(v_p5)

                    def s_out(sp, e=e, dt=dt, tt=tt, gi=gi, tok0=tok0):
                        sDV.wait(sp, C["p5_ev_%d_%d" % (e, gi)])
                        sl = slice(tok0 + tt * 512, tok0 + (tt + 1) * 512)
                        C["p5_od_%d_%d" % (e, gi)] = sOut.dma(
                            sp.dma_start(out=self.outT[dt * 128:(dt + 1) * 128, sl],
                                         in_=fstg[:, (gi % 2) * 512:(gi % 2) * 512 + 512]))
                    self.sp(s_out)


def _mk_consts():
    cid = np.eye(128, dtype=np.float32)
    ctri = (np.arange(128)[None, :] >= np.arange(128)[:, None]).astype(np.float32)
    return cid, ctri


def _core_inputs(core, inp):
    b = core
    e0 = 2 * (b % 4)
    cid, ctri = _mk_consts()
    A = np.ascontiguousarray
    d = {
        "xT": A(np.asarray(inp["L0_attn_inputs"])[b].T),
        "w_inT": A(np.asarray(inp["L0_attn_W_in"]).T),
        "convw": A(np.asarray(inp["L0_attn_conv_w"]).reshape(18, 128, 4).transpose(1, 0, 2).reshape(128, 72)),
        "convb": A(np.asarray(inp["L0_attn_conv_b"]).reshape(18, 128).T),
        "dtbias": A(np.asarray(inp["L0_attn_dt_bias"]).reshape(32, 1)),
        "alog": A(np.asarray(inp["L0_attn_A_log"]).reshape(32, 1)),
        "drepd": A(np.repeat(np.asarray(inp["L0_attn_D"]), 64).reshape(16, 128).T),
        "normwd": A(np.asarray(inp["L0_attn_norm_w"]).reshape(16, 128).T),
        "w_outT": A(np.asarray(inp["L0_attn_W_out"]).T),
        "lngd": A(np.asarray(inp["L0_ffn_ln_g"]).reshape(8, 128).T),
        "lnbd": A(np.asarray(inp["L0_ffn_ln_b"]).reshape(8, 128).T),
        "wq": A(inp["L0_ffn_wq"]), "wk": A(inp["L0_ffn_wk"]),
        "wv": A(inp["L0_ffn_wv"]), "wo": A(inp["L0_ffn_wo"]),
        "w1": A(np.asarray(inp["L0_ffn_w1"])[e0:e0 + 2]),
        "w2": A(np.asarray(inp["L0_ffn_w2"])[e0:e0 + 2]),
        "b1t": A(np.asarray(inp["L0_ffn_b1"])[e0:e0 + 2].reshape(2, 32, 128).transpose(0, 2, 1)),
        "b2t": A(np.asarray(inp["L0_ffn_b2"])[e0:e0 + 2].reshape(2, 8, 128).transpose(0, 2, 1)),
        "cid": cid, "ctri": ctri,
    }
    return {k: np.ascontiguousarray(np.asarray(v, dtype=np.float32)) for k, v in d.items()}


_prog_cache = {}


def _get_prog(debug=0, upto="all"):
    key = (debug, upto)
    if key not in _prog_cache:
        _prog_cache[key] = Builder(debug, upto).build()
    return _prog_cache[key]


def kernel(**inputs):
    nc, _ = _get_prog(_DEBUG, _UPTO)
    in_maps = [_core_inputs(b, inputs) for b in range(8)]
    res = run_bass_kernel_spmd(nc, in_maps, core_ids=list(range(8)))
    out = np.stack([np.ascontiguousarray(res.results[b]["outT"].T) for b in range(8)])
    return out.astype(np.float32)
